# version 02
# baseline (speedup 1.0000x reference)
import numpy as np
import contextlib
import math
import concourse.bass as bass
import concourse.mybir as mybir
from concourse.bass_utils import run_bass_kernel_spmd

F32 = mybir.dt.float32
BF16 = mybir.dt.bfloat16
U32 = mybir.dt.uint32
I32 = mybir.dt.int32
AF = mybir.ActivationFunctionType
ALU = mybir.AluOpType
AX = mybir.AxisListType

NCORES = 8
T = 4096
DM = 1024
NKT = DM // 128
EPS = 1e-5


class B:
    ENGINES = ("tensor", "vector", "scalar", "gpsimd", "sync")

    def __init__(self, nc, es):
        self.nc = nc
        self.es = es
        self.ops = {e: [] for e in self.ENGINES}
        self.sems = {}
        self.cnt = {}
        self.waited = {e: {} for e in self.ENGINES}
        self.sbuf_bytes = 0
        for e in self.ENGINES:
            self._mksem("q_" + e)

    def _mksem(self, name):
        self.sems[name] = self.es.enter_context(self.nc.semaphore(name))
        self.cnt[name] = 0

    def sbuf(self, name, shape, dtype, stack=None):
        t = (stack or self.es).enter_context(self.nc.sbuf_tensor(name, list(shape), dtype))
        self.sbuf_bytes += math.prod(shape[1:]) * mybir.dt.size(dtype)
        return t

    def psum(self, name, shape, dtype=F32):
        return self.es.enter_context(self.nc.psum_tensor(name, list(shape), dtype))

    def op(self, eng, fn, deps=(), sem=None, dma=False):
        semname = sem or ("q_" + eng)
        if semname not in self.sems:
            self._mksem(semname)
        amt = 16 if dma else 1
        best = {}
        for d in deps:
            if d is None:
                continue
            sname, val = d
            if val is None or val <= 0:
                continue
            if val > best.get(sname, 0):
                best[sname] = val
        waits = []
        weng = self.waited[eng]
        for sname, val in best.items():
            if val <= weng.get(sname, 0):
                continue
            waits.append((sname, val))
            weng[sname] = val
        self.cnt[semname] += amt
        tok = (semname, self.cnt[semname])
        self.ops[eng].append((waits, fn, semname, amt))
        return tok

    def now(self, eng_or_sem):
        name = eng_or_sem if eng_or_sem in self.cnt else "q_" + eng_or_sem
        return (name, self.cnt[name])

    def barrier_tokens(self):
        return [(n, v) for n, v in self.cnt.items() if v > 0]

    def build_block(self):
        block = self.es.enter_context(self.nc.Block())
        for e in self.ENGINES:
            lst = self.ops[e]
            if not lst:
                continue

            def emit(q, lst=lst):
                for waits, fn, semname, amt in lst:
                    for sname, val in waits:
                        q.wait_ge(self.sems[sname], val)
                    inst = fn(q)
                    if inst is not None:
                        inst.then_inc(self.sems[semname], amt)
            getattr(block, e)(emit)
        return block


class PsumMgr:

    def __init__(self, b, n=8):
        self.b = b
        self.t = [b.psum(f"PSB{i}", [128, 512]) for i in range(n)]
        self.readers = [[] for _ in range(n)]

    def war(self, i):
        return list(self.readers[i])

    def read(self, i, tok):
        self.readers[i].append(tok)

    def wrote(self, i):
        self.readers[i] = []


def dma(b, queue, out, in_, deps=(), sem=None, **kw):
    return b.op(queue, lambda q: q.dma_start(out=out, in_=in_, **kw),
                deps=deps, sem=sem, dma=True)


def mm(b, out, lhsT, rhs, start, stop, deps=(), sem=None):
    return b.op("tensor",
                lambda q: q.matmul(out, lhsT, rhs, start=start, stop=stop),
                deps=deps, sem=sem)


def acc_group(b, pm, bank, mk_mms, first_deps, out=None):
    deps0 = pm.war(bank) + list(first_deps)
    tok = None
    n = len(mk_mms)
    o = out if out is not None else pm.t[bank][:]
    for i, (lh, rh) in enumerate(mk_mms):
        tok = mm(b, o, lh, rh, start=(i == 0), stop=(i == n - 1),
                 deps=deps0 if i == 0 else [])
    pm.wrote(bank)
    return tok


def ln_stats(b, pm, C, src_f32, deps_src, bank_stats, bank_m, bank_r, W=512,
             nkt=NKT, dim=DM):
    t_stb = b.op("vector", lambda q: q.tensor_copy(C.stb[:, 0:nkt * W], src_f32),
                 deps=deps_src + pm.war(bank_stats))
    t_sqb = b.op("scalar", lambda q: q.activation(C.sqb[:, 0:nkt * W], src_f32,
                                                  AF.Square), deps=deps_src)
    ps = pm.t[bank_stats]
    tok_m = acc_group(b, pm, bank_stats,
                      [(C.onesc[:], C.stb[:, k * W:(k + 1) * W]) for k in range(nkt)],
                      [t_stb, C.t_ones], out=ps[0:1, 0:W])
    t_mean = b.op("scalar", lambda q: q.activation(
        C.rowsb[0:1, 0:W], ps[0:1, 0:W], AF.Copy, scale=1.0 / dim), deps=[tok_m])
    t_msq = b.op("scalar", lambda q: q.activation(
        C.rowf[0:1, 0*512:0*512 + W], ps[0:1, 0:W], AF.Square, scale=1.0 / dim), deps=[tok_m])
    pm.read(bank_stats, t_mean)
    pm.read(bank_stats, t_msq)
    tok_v = acc_group(b, pm, bank_stats,
                      [(C.onesc[:], C.sqb[:, k * W:(k + 1) * W]) for k in range(nkt)],
                      [t_sqb, t_mean, t_msq], out=ps[0:1, 0:W])
    t_var = b.op("vector", lambda q: q.tensor_scalar(
        C.rowf[0:1, 1*512:1*512 + W], ps[0:1, 0:W], 1.0 / dim, EPS, ALU.mult, op1=ALU.add),
        deps=[tok_v])
    pm.read(bank_stats, t_var)
    t_var2 = b.op("vector", lambda q: q.tensor_tensor(
        C.rowf[0:1, 1*512:1*512 + W], C.rowf[0:1, 1*512:1*512 + W], C.rowf[0:1, 0*512:0*512 + W], ALU.subtract),
        deps=[t_var, t_msq])
    t_sqr = b.op("scalar", lambda q: q.activation(
        C.rowf[0:1, 2*512:2*512 + W], C.rowf[0:1, 1*512:1*512 + W], AF.Sqrt), deps=[t_var2])
    t_rstd = b.op("vector", lambda q: q.reciprocal(C.rowf[0:1, 3*512:3*512 + W], C.rowf[0:1, 2*512:2*512 + W]),
                  deps=[t_sqr])
    t_rstdb = b.op("vector", lambda q: q.tensor_copy(
        C.rowsb[0:1, W:2 * W], C.rowf[0:1, 3*512:3*512 + W]), deps=[t_rstd])
    tok_bm = acc_group(b, pm, bank_m, [(C.onesr[:], C.rowsb[0:1, 0:W])],
                       [t_mean, C.t_onesr])
    tok_br = acc_group(b, pm, bank_r, [(C.onesr[:], C.rowsb[0:1, W:2 * W])],
                       [t_rstdb])
    return tok_bm, tok_br


class _C:
    pass


def build_launch_A():
    nc = bass.Bass("TRN2", target_bir_lowering=False)
    es = contextlib.ExitStack()
    b = B(nc, es)
    C = _C()

    xT = nc.dram_tensor("xT", [DM, T], F32, kind="ExternalInput").ap()
    wqT = nc.dram_tensor("wqT", [DM, 2048], F32, kind="ExternalInput").ap()
    wkvdT = nc.dram_tensor("wkvdT", [DM, 64], F32, kind="ExternalInput").ap()
    wkupT = nc.dram_tensor("wkupT", [64, 2048], F32, kind="ExternalInput").ap()
    wvupT = nc.dram_tensor("wvupT", [64, 2048], F32, kind="ExternalInput").ap()
    woT = nc.dram_tensor("woT", [2048, DM], F32, kind="ExternalInput").ap()
    g_col = nc.dram_tensor("g_col", [128, NKT], F32, kind="ExternalInput").ap()
    b_col = nc.dram_tensor("b_col", [128, NKT], F32, kind="ExternalInput").ap()
    wr_in = nc.dram_tensor("wr", [128, NKT * 7], F32, kind="ExternalInput").ap()
    identb_in = nc.dram_tensor("identb", [128, 128], BF16, kind="ExternalInput").ap()
    identf_in = nc.dram_tensor("identf", [128, 128], F32, kind="ExternalInput").ap()
    utrib_in = nc.dram_tensor("utrib", [128, 128], BF16, kind="ExternalInput").ap()

    h1T_out = nc.dram_tensor("h1T", [DM, T], F32, kind="ExternalOutput").ap()
    hT_out = nc.dram_tensor("hTb", [DM, T], BF16, kind="ExternalOutput").ap()
    htok_out = nc.dram_tensor("htok", [T, DM], BF16, kind="ExternalOutput").ap()
    route_out = nc.dram_tensor("route", [128, 6 * 32], F32, kind="ExternalOutput").ap()
    counts_out = nc.dram_tensor("counts", [1, 16], F32, kind="ExternalOutput").ap()

    outallT = nc.dram_tensor("outallT", [2048, T], BF16, kind="Internal").ap()
    qallT = nc.dram_tensor("qallT", [2048, T], BF16, kind="Internal").ap()

    latb = b.sbuf("latb", [64, T], BF16)
    C.rowf = b.sbuf("rowf", [1, 4 * 512], F32)
    C.rowsb = b.sbuf("rowsb", [1, 2 * 512], BF16)
    C.onesc = b.sbuf("onesc", [128, 1], BF16)
    C.onesr = b.sbuf("onesr", [1, 128], BF16)
    identb = b.sbuf("identb_sb", [128, 128], BF16)
    identf = b.sbuf("identf_sb", [128, 128], F32)
    utrib = b.sbuf("utrib_sb", [128, 128], BF16)
    gcol_sb = b.sbuf("gcol_sb", [128, NKT], F32)
    bcol_sb = b.sbuf("bcol_sb", [128, NKT], F32)
    wr_sb = b.sbuf("wr_sb", [128, NKT * 7], F32)
    lsb = b.sbuf("lsb", [7, 512], F32)
    ltok = b.sbuf("ltok", [128, 7], F32)
    iota7 = b.sbuf("iota7", [128, 7], F32)
    iota7d = b.sbuf("iota7d", [128, 7], F32)
    tk0 = b.sbuf("tk0", [128, 7], F32)
    tk1 = b.sbuf("tk1", [128, 7], F32)
    tk2 = b.sbuf("tk2", [128, 7], F32)
    ct = [b.sbuf(f"ct{i}", [128, 1], F32) for i in range(6)]
    ohb = b.sbuf("ohb", [128, 2 * 7], BF16)
    crun = b.sbuf("crun", [1, 16], F32)
    crunb = b.sbuf("crunb", [1, 16], BF16)
    utrif = b.sbuf("utrif", [128, 128], F32)
    onesrf = b.sbuf("onesrf", [1, 128], F32)
    onescf = b.sbuf("onescf", [128, 1], F32)
    pmat = b.sbuf("pmat", [128, 7], F32)
    stage = b.sbuf("stage", [128, 6 * 32], F32)
    rec = b.sbuf("rec", [128, 32], F32)

    pm = PsumMgr(b)

    t_id = dma(b, "sync", identb[:], identb_in, sem="ldc")
    t_if = dma(b, "sync", identf[:], identf_in, sem="ldc")
    t_ut = dma(b, "sync", utrib[:], utrib_in, sem="ldc")
    t_gc = dma(b, "sync", gcol_sb[:], g_col,
               sem="ldc")
    t_bc = dma(b, "sync", bcol_sb[:], b_col,
               sem="ldc")
    t_wr = dma(b, "sync", wr_sb[:], wr_in,
               sem="ldc")
    t_const = t_wr
    t_id = t_if = t_ut = t_gc = t_bc = t_wr = t_const
    C.t_ones = b.op("vector", lambda q: q.memset(C.onesc[:], 1.0))
    C.t_onesr = b.op("vector", lambda q: q.memset(C.onesr[:], 1.0))
    t_onesrf = b.op("vector", lambda q: q.memset(onesrf[:], 1.0))
    t_onescf = b.op("vector", lambda q: q.memset(onescf[:], 1.0))
    t_utf = b.op("vector", lambda q: q.tensor_copy(utrif[:], utrib[:]), deps=[t_ut])
    t_i7 = b.op("gpsimd", lambda q: q.iota(iota7[:], pattern=[[1, 7]], base=0,
                                           channel_multiplier=0,
                                           allow_small_or_imprecise_dtypes=True))
    t_i7d = b.op("gpsimd", lambda q: q.iota(iota7d[:], pattern=[[-1, 7]], base=7,
                                            channel_multiplier=0,
                                            allow_small_or_imprecise_dtypes=True))
    t_crun = b.op("vector", lambda q: q.memset(crun[:], 0.0))

    s1 = contextlib.ExitStack()
    xb = b.sbuf("xb", [128, NKT * T], BF16, stack=s1)
    wqs = [b.sbuf(f"wqs{i}", [128, NKT * 128], BF16, stack=s1) for i in range(2)]
    qstage = [b.sbuf(f"qstage{i}", [128, T], BF16, stack=s1) for i in range(2)]

    t_x = [dma(b, "gpsimd", xb[:, k * T:(k + 1) * T], xT[k * 128:(k + 1) * 128, :],
               sem="ldx") for k in range(NKT)]
    t_kvd = dma(b, "gpsimd", wkvd_list := None, None) if False else None
    wkvd_sb = b.sbuf("wkvd_sb", [128, NKT * 64], BF16, stack=s1)
    t_kvd = dma(b, "gpsimd", wkvd_sb[:].rearrange("p (kt f) -> p kt f", f=64), wkvdT.rearrange("(kt p) f -> p kt f", p=128),
                sem="ldkvd")

    lat_ev = []
    for ch in range(8):
        bank = ch % 2
        tokm = acc_group(b, pm, bank,
                         [(wkvd_sb[:, k * 64:(k + 1) * 64],
                           xb[:, k * T + ch * 512: k * T + (ch + 1) * 512])
                          for k in range(NKT)],
                         t_x + [t_kvd], out=pm.t[bank][0:64, :])
        ev = b.op("vector", lambda q, ch=ch, bank=bank: q.tensor_copy(
            latb[:, ch * 512:(ch + 1) * 512], pm.t[bank][0:64, :]), deps=[tokm])
        pm.read(bank, ev)
        lat_ev.append(ev)

    wq_tok = [None, None]
    wq_tok[0] = dma(b, "gpsimd", wqs[0][:].rearrange("p (kt f) -> p kt f", f=128),
                    wqT[:, 0:128].rearrange("(kt p) f -> p kt f", p=128), sem="wqA")
    qst_store = [None, None]
    last_stq = None
    for f in range(16):
        buf = f % 2
        if f + 1 < 16:
            wq_tok[(f + 1) % 2] = dma(
                b, "gpsimd", wqs[(f + 1) % 2][:].rearrange("p (kt f) -> p kt f", f=128),
                wqT[:, (f + 1) * 128:(f + 2) * 128].rearrange("(kt p) f -> p kt f", p=128),
                deps=[b.now("q_tensor")], sem=("wqA" if (f + 1) % 2 == 0 else "wqB"))
        evs = []
        for ch in range(8):
            bank = 2 + (ch % 2)
            tokm = acc_group(b, pm, bank,
                             [(wqs[buf][:, k * 128:(k + 1) * 128],
                               xb[:, k * T + ch * 512: k * T + (ch + 1) * 512])
                              for k in range(NKT)],
                             [wq_tok[buf], t_x[-1]])
            ev = b.op("vector", lambda q, ch=ch, buf=buf, bank=bank: q.tensor_copy(
                qstage[buf][:, ch * 512:(ch + 1) * 512], pm.t[bank][:]),
                deps=[tokm] + ([last_stq] if last_stq else []))
            pm.read(bank, ev)
            evs.append(ev)
        qst_store[buf] = last_stq = dma(
            b, "sync", qallT[f * 128:(f + 1) * 128, :],
            qstage[buf][:], deps=evs, sem="stq")
    p1_done = [b.now("q_tensor"), b.now("q_vector"), qst_store[0], qst_store[1]]
    s1.close()

    s2 = contextlib.ExitStack()
    wkup_sb = b.sbuf("wkup_sb", [64, 2048], BF16, stack=s2)
    wvup_sb = b.sbuf("wvup_sb", [64, 2048], BF16, stack=s2)
    qt = b.sbuf("qt", [128, 2 * T], BF16, stack=s2)
    kt_sb = b.sbuf("kt_sb", [128, 2 * T], BF16, stack=s2)
    vb = b.sbuf("vb", [128, 32 * 257], BF16, stack=s2)
    ptb = b.sbuf("ptb", [128, 32 * 512], BF16, stack=s2)
    onb = b.sbuf("onb", [128, 32 * 256], BF16, stack=s2)
    ontT = b.sbuf("ontT", [128, 2 * T], BF16, stack=s2)

    t_kup = dma(b, "gpsimd", wkup_sb[:], wkupT, deps=p1_done, sem="ldkup")
    t_vup = dma(b, "gpsimd", wvup_sb[:], wvupT, deps=p1_done, sem="ldvup")
    head_store = []
    for h in range(8):
        head_dep = [b.now("q_tensor"), b.now("q_scalar")]
        t_q = [dma(b, "sync", qt[:, mt * T:(mt + 1) * T],
                   qallT[h * 256 + mt * 128: h * 256 + (mt + 1) * 128, :],
                   deps=p1_done + head_dep, sem="ldq") for mt in range(2)]
        qk_ev = []
        g = 0
        for mt in range(2):
            for ch in range(8):
                bank = g % 2
                tokm = acc_group(b, pm, bank,
                                 [(wkup_sb[:, h * 256 + mt * 128: h * 256 + (mt + 1) * 128],
                                   latb[:, ch * 512:(ch + 1) * 512])],
                                 [t_kup, lat_ev[-1]])
                ev = b.op("vector", lambda q, mt=mt, ch=ch, bank=bank: q.tensor_copy(
                    kt_sb[:, mt * T + ch * 512: mt * T + (ch + 1) * 512],
                    pm.t[bank][:]), deps=[tokm] + (head_dep if g == 0 else []))
                pm.read(bank, ev)
                qk_ev.append(ev)
                g += 1
        t_vones = b.op("vector", lambda q: q.memset(
            vb[:].rearrange("p (tt c) -> p tt c", c=257)[:, :, 256:257], 1.0),
            deps=head_dep)
        v_ev = [t_vones]
        for tt in range(32):
            bank = 2 + (tt % 2)
            tokm = acc_group(b, pm, bank,
                             [(latb[:, tt * 128:(tt + 1) * 128],
                               wvup_sb[:, h * 256:(h + 1) * 256])],
                             [t_vup, lat_ev[-1]] + (head_dep if tt < 2 else []),
                             out=pm.t[bank][:, 0:256])
            ev = b.op("vector", lambda q, tt=tt, bank=bank: q.tensor_copy(
                vb[:, tt * 257: tt * 257 + 256], pm.t[bank][:, 0:256]), deps=[tokm])
            pm.read(bank, ev)
            v_ev.append(ev)
        qkv_done = qk_ev + v_ev + t_q

        last_av = None
        for ch in range(8):
            exp_toks = []
            for kt32 in range(32):
                bank = 2 + (kt32 % 2)
                tokm = acc_group(
                    b, pm, bank,
                    [(kt_sb[:, dht * T + kt32 * 128: dht * T + (kt32 + 1) * 128],
                      qt[:, dht * T + ch * 512: dht * T + (ch + 1) * 512])
                     for dht in range(2)],
                    qkv_done if (kt32 < 2 and ch == 0) else [])
                ev = b.op("scalar", lambda q, kt32=kt32, bank=bank: q.activation(
                    ptb[:, kt32 * 512:(kt32 + 1) * 512], pm.t[bank][:], AF.Exp,
                    scale=1.0 / 16.0),
                    deps=[tokm] + ([last_av] if (kt32 == 0 and last_av) else []))
                pm.read(bank, ev)
                exp_toks.append(ev)
            for qt4 in range(4):
                qtile = ch * 4 + qt4
                deps0 = pm.war(4) + [exp_toks[-1]] + (v_ev if qtile == 0 else [])
                tokm = None
                for kt32 in range(32):
                    tokm = mm(b, pm.t[4][:, 0:257],
                              ptb[:, kt32 * 512 + qt4 * 128: kt32 * 512 + (qt4 + 1) * 128],
                              vb[:, kt32 * 257:(kt32 + 1) * 257],
                              start=(kt32 == 0), stop=(kt32 == 31),
                              deps=deps0 if kt32 == 0 else [])
                pm.wrote(4)
                last_av = tokm
                t_rc = b.op("vector", lambda q, qtile=qtile: q.reciprocal(
                    rec[:, qtile:qtile + 1], pm.t[4][:, 256:257]), deps=[tokm])
                t_on = b.op("scalar", lambda q, qtile=qtile: q.activation(
                    onb[:, qtile * 256:(qtile + 1) * 256], pm.t[4][:, 0:256], AF.Copy,
                    scale=rec[:, qtile:qtile + 1]), deps=[t_rc])
                pm.read(4, t_on)
        attn_done = [b.now("q_scalar"), b.now("q_vector")]
        tr_last = None
        for tt in range(32):
            for dht in range(2):
                bank = 5 + ((tt * 2 + dht) % 2)
                tokm = acc_group(b, pm, bank,
                                 [(onb[:, tt * 256 + dht * 128: tt * 256 + (dht + 1) * 128],
                                   identb[:])],
                                 attn_done + [t_id] +
                                 (head_store[-2:] if (tt == 0 and dht == 0) else []),
                                 out=pm.t[bank][:, 0:128])
                ev = b.op("vector", lambda q, tt=tt, dht=dht, bank=bank: q.tensor_copy(
                    ontT[:, dht * T + tt * 128: dht * T + (tt + 1) * 128],
                    pm.t[bank][:, 0:128]), deps=[tokm])
                pm.read(bank, ev)
                tr_last = ev
        sA = dma(b, "sync", outallT[h * 256: h * 256 + 128, :], ontT[:, 0:T],
                 deps=[tr_last], sem="sto")
        sB = dma(b, "sync", outallT[h * 256 + 128: h * 256 + 256, :], ontT[:, T:2 * T],
                 deps=[tr_last], sem="sto")
        head_store += [sA, sB]
    p2_done = [b.now("q_tensor"), b.now("q_vector"), b.now("q_scalar")] + head_store[-2:]
    s2.close()

    s3 = contextlib.ExitStack()
    wo_sb = b.sbuf("wo_sb", [128, 16 * DM], BF16, stack=s3)
    oall = b.sbuf("oall", [128, 16 * 512], BF16, stack=s3)
    xf = b.sbuf("xf", [128, NKT * 512], F32, stack=s3)
    st = b.sbuf("st", [128, NKT * 512], F32, stack=s3)
    C.stb = b.sbuf("stb", [128, NKT * 512], BF16, stack=s3)
    C.sqb = b.sbuf("sqb", [128, NKT * 512], BF16, stack=s3)
    h1b = b.sbuf("h1b", [128, NKT * 512], F32, stack=s3)
    hbuf = b.sbuf("hbuf", [128, NKT * 512], F32, stack=s3)
    hb16 = b.sbuf("hb16", [128, NKT * 512], BF16, stack=s3)
    htokst = b.sbuf("htokst", [128, 4 * DM], BF16, stack=s3)
    assert b.sbuf_bytes < 400 * 1024

    t_wo = dma(b, "gpsimd", wo_sb[:].rearrange("p (kt f) -> p kt f", f=DM),
               woT.rearrange("(kt p) f -> p kt f", p=128),
               deps=p2_done, sem="ldwo")
    tile_idx = 0
    prev_cu = None
    prev_store = []
    for ch in range(8):
        cdep = (p2_done + [b.now("q_tensor"), b.now("q_vector"), b.now("q_scalar")]
                + prev_store)
        t_oall = dma(b, "sync", oall[:].rearrange("p (kt t) -> p kt t", t=512),
                     outallT[:, ch * 512:(ch + 1) * 512].rearrange(
                         "(kt p) t -> p kt t", p=128),
                     deps=cdep, sem="ld3")
        t_xf = dma(b, "sync", xf[:].rearrange("p (kt t) -> p kt t", t=512),
                   xT[:, ch * 512:(ch + 1) * 512].rearrange("(kt p) t -> p kt t", p=128),
                   deps=cdep, sem="ld3")
        st_toks = []
        for ft in range(NKT):
            bank = ft % 4
            tokm = acc_group(b, pm, bank,
                             [(wo_sb[:, k * DM + ft * 128: k * DM + (ft + 1) * 128],
                               oall[:, k * 512:(k + 1) * 512]) for k in range(16)],
                             [t_oall, t_xf, t_wo])
            ev = b.op("vector", lambda q, ft=ft, bank=bank: q.tensor_tensor(
                st[:, ft * 512:(ft + 1) * 512], pm.t[bank][:],
                xf[:, ft * 512:(ft + 1) * 512], ALU.add), deps=[tokm, t_xf])
            pm.read(bank, ev)
            st_toks.append(ev)
        bm, br = ln_stats(b, pm, C, st[:], st_toks, 7, 4, 5)
        h1_toks = []
        for ft in range(NKT):
            a1 = b.op("vector", lambda q, ft=ft: q.tensor_tensor(
                h1b[:, ft * 512:(ft + 1) * 512], st[:, ft * 512:(ft + 1) * 512],
                pm.t[4][:], ALU.subtract), deps=[bm] + st_toks)
            a2 = b.op("vector", lambda q, ft=ft: q.tensor_tensor(
                h1b[:, ft * 512:(ft + 1) * 512], h1b[:, ft * 512:(ft + 1) * 512],
                pm.t[5][:], ALU.mult), deps=[br, a1])
            pm.read(4, a1)
            pm.read(5, a2)
            h1_toks.append(a2)
        s_h1 = dma(b, "sync", h1T_out[:, ch * 512:(ch + 1) * 512].rearrange(
            "(kt p) t -> p kt t", p=128),
            h1b[:].rearrange("p (kt t) -> p kt t", t=512), deps=h1_toks, sem="sth")
        bm2, br2 = ln_stats(b, pm, C, h1b[:], h1_toks, 7, 4, 5)
        h_toks = []
        for ft in range(NKT):
            a1 = b.op("vector", lambda q, ft=ft: q.tensor_tensor(
                hbuf[:, ft * 512:(ft + 1) * 512], h1b[:, ft * 512:(ft + 1) * 512],
                pm.t[4][:], ALU.subtract), deps=[bm2] + h1_toks)
            a2 = b.op("vector", lambda q, ft=ft: q.tensor_tensor(
                hbuf[:, ft * 512:(ft + 1) * 512], hbuf[:, ft * 512:(ft + 1) * 512],
                pm.t[5][:], ALU.mult), deps=[br2, a1])
            a3 = b.op("vector", lambda q, ft=ft: q.tensor_scalar(
                hbuf[:, ft * 512:(ft + 1) * 512], hbuf[:, ft * 512:(ft + 1) * 512],
                gcol_sb[:, ft:ft + 1], bcol_sb[:, ft:ft + 1], ALU.mult, op1=ALU.add),
                deps=[a2, t_gc, t_bc])
            pm.read(4, a1)
            pm.read(5, a2)
            h_toks.append(a3)
        t_hb16 = b.op("vector", lambda q: q.tensor_copy(hb16[:], hbuf[:]), deps=h_toks)
        s_hT = dma(b, "sync", hT_out[:, ch * 512:(ch + 1) * 512].rearrange(
            "(kt p) t -> p kt t", p=128),
            hb16[:].rearrange("p (kt t) -> p kt t", t=512), deps=[t_hb16], sem="sth")
        tr_toks = []
        for tt4 in range(4):
            for ft in range(NKT):
                bank = (tt4 * NKT + ft) % 2
                tokm = acc_group(b, pm, bank,
                                 [(hb16[:, ft * 512 + tt4 * 128: ft * 512 + (tt4 + 1) * 128],
                                   identb[:])], [t_hb16, t_id],
                                 out=pm.t[bank][:, 0:128])
                ev = b.op("vector", lambda q, tt4=tt4, ft=ft, bank=bank: q.tensor_copy(
                    htokst[:, tt4 * DM + ft * 128: tt4 * DM + (ft + 1) * 128],
                    pm.t[bank][:, 0:128]), deps=[tokm])
                pm.read(bank, ev)
                tr_toks.append(ev)
        s_htok = dma(b, "sync", htok_out[ch * 512:(ch + 1) * 512, :].rearrange(
            "(tt p) d -> p tt d", p=128),
            htokst[:].rearrange("p (tt d) -> p tt d", d=DM), deps=tr_toks, sem="sth")
        tok_l = acc_group(b, pm, 6,
                          [(wr_sb[:, k * 7:(k + 1) * 7],
                            hbuf[:, k * 512:(k + 1) * 512]) for k in range(NKT)],
                          h_toks + [t_wr], out=pm.t[6][0:7, :])
        t_lsb = b.op("vector", lambda q: q.tensor_copy(lsb[:], pm.t[6][0:7, :]),
                     deps=[tok_l])
        pm.read(6, t_lsb)
        for tt4 in range(4):
            tokm = acc_group(b, pm, 7, [(lsb[0:7, tt4 * 128:(tt4 + 1) * 128],
                                         identf[0:7, 0:7])],
                             [t_lsb, t_if] + ([prev_cu] if prev_cu else []),
                             out=pm.t[7][:, 0:7])
            t_lt = b.op("vector", lambda q: q.tensor_copy(ltok[:], pm.t[7][:, 0:7]),
                        deps=[tokm])
            pm.read(7, t_lt)
            t_v1 = b.op("vector", lambda q: q.reduce_max(ct[0][:], ltok[:], axis=AX.X),
                        deps=[t_lt])
            t_eq1 = b.op("vector", lambda q: q.tensor_scalar(
                tk0[:], ltok[:], ct[0][:], None, ALU.is_equal), deps=[t_v1, t_i7])
            t_m1 = b.op("vector", lambda q: q.tensor_tensor(
                tk1[:], tk0[:], iota7d[:], ALU.mult), deps=[t_eq1, t_i7d])
            t_m1r = b.op("vector", lambda q: q.reduce_max(ct[1][:], tk1[:], axis=AX.X),
                         deps=[t_m1])
            t_i1 = b.op("vector", lambda q: q.tensor_scalar(
                ct[1][:], ct[1][:], -1.0, 7.0, ALU.mult, op1=ALU.add), deps=[t_m1r])
            t_oh1 = b.op("vector", lambda q: q.tensor_scalar(
                tk0[:], iota7[:], ct[1][:], None, ALU.is_equal), deps=[t_i1])
            t_l2 = b.op("vector", lambda q: q.scalar_tensor_tensor(
                tk2[:], tk0[:], -1e30, ltok[:], ALU.mult, ALU.add), deps=[t_oh1])
            t_v2 = b.op("vector", lambda q: q.reduce_max(ct[2][:], tk2[:], axis=AX.X),
                        deps=[t_l2])
            t_eq2 = b.op("vector", lambda q: q.tensor_scalar(
                tk1[:], tk2[:], ct[2][:], None, ALU.is_equal), deps=[t_v2])
            t_m2 = b.op("vector", lambda q: q.tensor_tensor(
                tk1[:], tk1[:], iota7d[:], ALU.mult), deps=[t_eq2])
            t_m2r = b.op("vector", lambda q: q.reduce_max(ct[3][:], tk1[:], axis=AX.X),
                         deps=[t_m2])
            t_i2 = b.op("vector", lambda q: q.tensor_scalar(
                ct[3][:], ct[3][:], -1.0, 7.0, ALU.mult, op1=ALU.add), deps=[t_m2r])
            t_oh2 = b.op("vector", lambda q: q.tensor_scalar(
                tk1[:], iota7[:], ct[3][:], None, ALU.is_equal), deps=[t_i2])
            t_d = b.op("vector", lambda q: q.tensor_tensor(
                ct[4][:], ct[2][:], ct[0][:], ALU.subtract), deps=[t_v2, t_eq1])
            t_e = b.op("scalar", lambda q: q.activation(
                ct[4][:], ct[4][:], AF.Exp), deps=[t_d])
            t_e1 = b.op("vector", lambda q: q.tensor_scalar(
                ct[4][:], ct[4][:], 1.0, None, ALU.add), deps=[t_e])
            t_g1 = b.op("vector", lambda q: q.reciprocal(ct[4][:], ct[4][:]),
                        deps=[t_e1])
            t_g2 = b.op("vector", lambda q: q.tensor_scalar(
                ct[5][:], ct[4][:], -1.0, 1.0, ALU.mult, op1=ALU.add), deps=[t_g1])
            p_toks = {}
            for k2 in range(2):
                src_oh = tk0 if k2 == 0 else tk1
                t_dep = t_oh1 if k2 == 0 else t_oh2
                deps0 = pm.war(7) + [t_dep, t_utf, t_onesrf,
                                     ([prev_cu] if prev_cu else [t_crun])[0]]
                mm(b, pm.t[7][:, 0:7], utrif[:], src_oh[:], start=True, stop=False,
                   deps=deps0)
                tk_p2 = mm(b, pm.t[7][:, 0:7], onesrf[:],
                           crun[0:1, k2 * 7:(k2 + 1) * 7], start=False, stop=True,
                           deps=[])
                pm.wrote(7)
                t_pm = b.op("vector", lambda q: q.tensor_copy(pmat[:], pm.t[7][:, 0:7]),
                            deps=[tk_p2])
                pm.read(7, t_pm)
                t_ps = b.op("vector", lambda q, src_oh=src_oh: q.tensor_tensor(
                    pmat[:], pmat[:], src_oh[:], ALU.mult), deps=[t_pm, t_dep])
                col = ct[0] if k2 == 0 else ct[2]
                p_toks[k2] = b.op("vector", lambda q, col=col: q.reduce_sum(
                    col[:], pmat[:], axis=AX.X), deps=[t_ps])
                deps_c = pm.war(7) + [t_dep, t_onescf]
                tk_c = acc_group(b, pm, 7, [(onescf[:], src_oh[:])], deps_c,
                                 out=pm.t[7][0:1, 0:7])
                t_cu = b.op("vector", lambda q, k2=k2: q.tensor_tensor(
                    crun[0:1, k2 * 7:(k2 + 1) * 7], crun[0:1, k2 * 7:(k2 + 1) * 7],
                    pm.t[7][0:1, 0:7], ALU.add), deps=[tk_c, p_toks[k2]])
                pm.read(7, t_cu)
                prev_cu = t_cu
            quant = [(0, ct[1], t_oh1), (1, ct[3], t_oh2),
                     (2, ct[4], t_g2), (3, ct[5], t_g2),
                     (4, ct[0], p_toks[0]), (5, ct[2], p_toks[1])]
            for qi, src, tk in quant:
                b.op("vector", lambda q, qi=qi, src=src, ti=tile_idx: q.tensor_copy(
                    stage[:, qi * 32 + ti: qi * 32 + ti + 1], src[:]), deps=[tk])
            tile_idx += 1
        prev_store = [s_h1, s_hT, s_htok]
    s3.close()

    s_route = dma(b, "sync", route_out, stage[:], deps=[b.now("q_vector")], sem="sth")
    s_counts = dma(b, "sync", counts_out, crun[:], deps=[b.now("q_vector")], sem="sth")
    b.op("sync", lambda q: q.nop(), deps=b.barrier_tokens())
    b.build_block()
    return nc, es



CAP = 2926
CAPP = 2944
NSLOT = 7 * CAPP
SEG = 384


def v3(t, sh, strides, off=0):
    ap = t.ap() if hasattr(t, "ap") else t
    return bass.AP(ap.tensor, ap.offset + off, [list(s) for s in zip(strides, sh)])


def build_launch_B():
    nc = bass.Bass("TRN2", target_bir_lowering=False, detect_race_conditions=False)
    es = contextlib.ExitStack()
    b = B(nc, es)

    htok_all = nc.dram_tensor("htok_all", [NCORES * T, DM], BF16, kind="ExternalInput").ap()
    hT = nc.dram_tensor("hT", [DM, T], BF16, kind="ExternalInput").ap()
    IAll = nc.dram_tensor("IAll", [128, 512], F32, kind="ExternalInput").ap()
    PAll = nc.dram_tensor("PAll", [128, 512], F32, kind="ExternalInput").ap()
    GAll = nc.dram_tensor("GAll", [128, 512], F32, kind="ExternalInput").ap()
    counts_in = nc.dram_tensor("counts_all", [8, 16], F32, kind="ExternalInput").ap()
    mycols_in = nc.dram_tensor("mycols", [128, 1], U32, kind="ExternalInput").ap()
    w1_in = nc.dram_tensor("w1e", [7, DM, 4096], F32, kind="ExternalInput").ap()
    w3_in = nc.dram_tensor("w3e", [7, DM, 4096], F32, kind="ExternalInput").ap()
    w2_in = nc.dram_tensor("w2e", [7, 4096, DM], F32, kind="ExternalInput").ap()
    w1s_in = nc.dram_tensor("w1s", [DM, 4096], F32, kind="ExternalInput").ap()
    w3s_in = nc.dram_tensor("w3s", [DM, 4096], F32, kind="ExternalInput").ap()
    w2s_in = nc.dram_tensor("w2s", [4096, DM], F32, kind="ExternalInput").ap()
    identb_in = nc.dram_tensor("identb", [128, 128], BF16, kind="ExternalInput").ap()
    identf_in = nc.dram_tensor("identf", [128, 128], F32, kind="ExternalInput").ap()
    utrib_in = nc.dram_tensor("utrib", [128, 128], BF16, kind="ExternalInput").ap()

    y_out = nc.dram_tensor("y_out", [7 * SEG, DM], F32, kind="ExternalOutput").ap()
    sharedT_out = nc.dram_tensor("sharedT", [DM, T], F32, kind="ExternalOutput").ap()
    u_all_out = nc.dram_tensor("u_all", [128, 512], F32, kind="ExternalOutput").ap()
    w_all_out = nc.dram_tensor("w_all", [128, 512], F32, kind="ExternalOutput").ap()

    mapd = nc.dram_tensor("mapd", [NSLOT, 1], F32, kind="Internal").ap()

    pm = PsumMgr(b)

    identb = b.sbuf("identb_sb", [128, 128], BF16)
    identf = b.sbuf("identf_sb", [128, 128], F32)
    utrif = b.sbuf("utrif_sb", [128, 128], F32)
    utrib = b.sbuf("utrib_sb", [128, 128], BF16)
    cnt_sb = b.sbuf("cnt_sb", [8, 16], F32)
    brow = b.sbuf("brow", [1, 128], F32)
    onesrf = b.sbuf("onesrf", [1, 128], F32)
    onescf = b.sbuf("onescf", [128, 1], F32)
    i7r = b.sbuf("i7r", [128, 7], F32)
    midx = b.sbuf("midx", [128, 1], U32)
    gidx = b.sbuf("gidx", [128, 128], U32)
    hch = b.sbuf("hch", [128, NKT * 512], BF16)
    w1b = [b.sbuf(f"w1b{i}", [128, NKT * 512], BF16) for i in range(2)]
    w3b = [b.sbuf(f"w3b{i}", [128, NKT * 512], BF16) for i in range(2)]
    w2b = [b.sbuf(f"w2b{i}", [128, 32 * 128], BF16) for i in range(2)]
    sa = b.sbuf("sa", [128, 32 * 512], BF16)
    sbb = b.sbuf("sbb", [128, 32 * 512], BF16)
    gg = b.sbuf("gg", [128, 32 * 512], BF16)
    ysh = b.sbuf("ysh", [128, NKT * 512], F32)
    sB1 = contextlib.ExitStack()
    iAll_sb = b.sbuf("iAll_sb", [128, 512], F32, stack=sB1)
    pAll_sb = b.sbuf("pAll_sb", [128, 512], F32, stack=sB1)
    gAll_sb = b.sbuf("gAll_sb", [128, 512], F32, stack=sB1)
    u_sb = b.sbuf("u_sb", [128, 512], F32, stack=sB1)
    wk_sb = b.sbuf("wk_sb", [128, 512], F32, stack=sB1)
    uu_sb = b.sbuf("uu_sb", [128, 512], U32, stack=sB1)
    oh8 = b.sbuf("oh8", [128, 56], F32, stack=sB1)
    scr = b.sbuf("scr", [128, 64], F32, stack=sB1)
    zrow = b.sbuf("zrow", [128, 164], F32, stack=sB1)
    mrows = b.sbuf("mrows", [128, 128], F32, stack=sB1)
    mcolsT = b.sbuf("mcolsT", [128, 128], F32, stack=sB1)
    bb7 = b.sbuf("bb7", [128, 7], F32, stack=sB1)
    tokbufs = [b.sbuf(f"tokb{i}", [128, 8], F32, stack=sB1) for i in range(8)]

    for ap_sb, ap_in in [(identb[:], identb_in), (identf[:], identf_in),
                         (utrib[:], utrib_in), (cnt_sb[:], counts_in),
                         (iAll_sb[:], IAll), (pAll_sb[:], PAll),
                         (gAll_sb[:], GAll)]:
        dma(b, "sync", ap_sb, ap_in, sem="ldc")
    t_const = dma(b, "sync", midx[:], mycols_in, sem="ldc")
    t_utf = b.op("vector", lambda q: q.tensor_copy(utrif[:], utrib[:]), deps=[t_const])
    t_or = b.op("vector", lambda q: q.memset(onesrf[:], 1.0))
    t_oc = b.op("vector", lambda q: q.memset(onescf[:], 1.0))
    t_zr = b.op("vector", lambda q: q.memset(zrow[:], 0.0))
    t_mz = b.op("vector", lambda q: q.memset(mrows[:], 0.0))
    t_i7 = b.op("gpsimd", lambda q: q.iota(i7r[:], pattern=[[1, 7]], base=0,
               channel_multiplier=0, allow_small_or_imprecise_dtypes=True))
    b.regs = {}

    def _init_regs(q):
        r1 = q.alloc_register()
        q.reg_mov(r1, NSLOT - 1)
        r2 = q.alloc_register()
        q.reg_mov(r2, 160)
        r3 = q.alloc_register()
        b.regs["slot"] = r1
        b.regs["col"] = r2
        b.regs["tok"] = r3
        return q.reg_mov(r3, NCORES * T - 1)
    t_regs = b.op("gpsimd", _init_regs)
    t_consts = [t_const, t_utf, t_or, t_oc, t_zr, t_i7, t_regs, t_mz]

    t_zmap = dma(b, "gpsimd", mapd.rearrange("(p cc) one -> p (cc one)", p=128),
                 zrow[:, 0:161], deps=[t_zr], sem="zm")

    base_toks = []
    for c in range(8):
        tok1 = acc_group(b, pm, 6, [(utrif[0:8, c:c + 1], cnt_sb[0:8, 0:16])],
                         [t_utf, t_const], out=pm.t[6][0:1, 0:16])
        e1 = b.op("vector", lambda q, c=c: q.tensor_copy(
            brow[0:1, c * 7:c * 7 + 7], pm.t[6][0:1, 0:7]), deps=[tok1])
        e2 = b.op("vector", lambda q, c=c: q.tensor_copy(
            brow[0:1, 56 + c * 7:56 + c * 7 + 7], pm.t[6][0:1, 7:14]), deps=[tok1])
        pm.read(6, e1)
        pm.read(6, e2)
        base_toks += [e1, e2]
    tokt = acc_group(b, pm, 6, [(onescf[0:8, 0:1], cnt_sb[0:8, 0:7])],
                     [t_oc] + base_toks, out=pm.t[6][0:1, 0:7])
    e3 = b.op("vector", lambda q: q.tensor_copy(brow[0:1, 112:119], pm.t[6][0:1, 0:7]),
              deps=[tokt])
    pm.read(6, e3)
    t_base = e3
    for c in range(8):
        t_base = b.op("vector", lambda q, c=c: q.tensor_tensor(
            brow[0:1, 56 + c * 7:56 + c * 7 + 7], brow[0:1, 56 + c * 7:56 + c * 7 + 7],
            brow[0:1, 112:119], ALU.add), deps=[e3])

    map_dve = []
    t_bb = None
    for bi in range(64):
        k2 = bi // 32
        cc = (bi // 4) % 8
        j0 = bi * 8
        sl = slice(j0, j0 + 8)
        dep0 = t_consts + [t_base] if bi == 0 else []
        if bi % 4 == 0:
            off = k2 * 56 + cc * 7
            tokb = acc_group(b, pm, 6, [(onesrf[:], brow[0:1, off:off + 7])],
                             [t_base, t_or] + dep0, out=pm.t[6][:, 0:7])
            t_bb = b.op("vector", lambda q: q.tensor_copy(bb7[:], pm.t[6][:, 0:7]),
                        deps=[tokb])
            pm.read(6, t_bb)
        t1 = b.op("vector", lambda q, j0=j0: q.tensor_tensor(
            v3(oh8, (128, 8, 7), (56, 7, 1)),
            bass.AP(iAll_sb.ap().tensor, iAll_sb.ap().offset + j0,
                    [[512, 128], [1, 8], [0, 7]]),
            v3(i7r, (128, 8, 7), (7, 0, 1)), ALU.is_equal), deps=dep0 + [t_bb])
        t2 = b.op("vector", lambda q: q.tensor_tensor(
            v3(scr, (128, 8, 7), (64, 7, 1)),
            v3(oh8, (128, 8, 7), (56, 7, 1)),
            v3(bb7, (128, 8, 7), (7, 0, 1)), ALU.mult), deps=[t1, t_bb])
        t3 = b.op("vector", lambda q: q.tensor_reduce(
            v3(scr, (128, 8, 1), (64, 1, 1), off=56),
            v3(scr, (128, 8, 7), (64, 7, 1)), axis=AX.X, op=ALU.add), deps=[t2])
        sel8 = scr[:, 56:64]
        pos8 = scr[:, 0:8]
        t4 = b.op("vector", lambda q, sl=sl: q.tensor_tensor(
            pos8, pAll_sb[:, sl], sel8, ALU.add), deps=[t3])
        keep8 = scr[:, 8:16]
        t5 = b.op("vector", lambda q: q.tensor_scalar(
            keep8, pos8, float(CAP), None, ALU.is_lt), deps=[t4])
        t6 = b.op("vector", lambda q, sl=sl: q.tensor_tensor(
            wk_sb[:, sl], gAll_sb[:, sl], keep8, ALU.mult), deps=[t5])
        u8 = scr[:, 16:24]
        t7 = b.op("vector", lambda q, sl=sl: q.scalar_tensor_tensor(
            u8, iAll_sb[:, sl], float(CAPP), pos8, ALU.mult, ALU.add), deps=[t4])
        t8 = b.op("vector", lambda q: q.tensor_scalar(
            scr[:, 24:32], keep8, -1.0, 1.0, ALU.mult, op1=ALU.add), deps=[t7, t5])
        t9 = b.op("vector", lambda q, sl=sl: q.scalar_tensor_tensor(
            u_sb[:, sl], scr[:, 24:32], 1e9, u8, ALU.mult, ALU.add), deps=[t8])
        t16 = b.op("vector", lambda q, sl=sl: q.tensor_copy(uu_sb[:, sl], u_sb[:, sl]),
                   deps=[t9])
        map_dve.append((t16, None, sl))
    t_u_done = map_dve[-1][0]
    s_u = dma(b, "sync", u_all_out, u_sb[:], deps=[t_u_done], sem="sou")
    s_w = dma(b, "sync", w_all_out, wk_sb[:], deps=[t_u_done], sem="sou")

    scat_state = {"toks": [], "iota_toks": {}}

    def emit_scatters(bia, bib):
        for bi in range(bia, bib):
            t16, t17, sl = map_dve[bi]
            buf = tokbufs[bi % 8]
            k2 = bi // 32
            cc = (bi // 4) % 8
            tbase = cc * T + (bi % 4) * 1024 + 1
            prev = scat_state["iota_toks"].get(bi % 8)
            ti = b.op("gpsimd", lambda q, tbase=tbase, buf=buf: q.iota(
                buf[:], pattern=[[128, 8]], base=tbase, channel_multiplier=1,
                allow_small_or_imprecise_dtypes=True),
                deps=[t16] + ([prev] if prev else []) + ([t_zmap, t_regs] if bi == 0 else []))
            toks = []
            for jj in range(8):
                sc = b.op("gpsimd", lambda q, jj=jj, sl=sl, buf=buf: q.indirect_dma_start(
                    out=mapd,
                    out_offset=bass.IndirectOffsetOnAxis(
                        ap=uu_sb[:, sl.start + jj:sl.start + jj + 1], axis=0),
                    in_=buf[:, jj:jj + 1], in_offset=None,
                    bounds_check=b.regs["slot"], oob_is_err=False),
                    deps=[ti], sem="scat", dma=True)
                toks.append(sc)
            scat_state["iota_toks"][bi % 8] = toks[-1]
            scat_state["toks"] += toks

    w1tok = [None, None]
    w3tok = [None, None]
    w2tok = [None, None]
    sh_store = []
    for ci in range(8):
        t_h = dma(b, "sync", hch[:].rearrange("p (kt t) -> p kt t", t=512),
                  hT[:, ci * 512:(ci + 1) * 512].rearrange("(kt p) t -> p kt t", p=128),
                  deps=[b.now("q_tensor")], sem="ldh")
        emit_scatters(ci * 8, (ci + 1) * 8)
        if ci > 0:
            pass
        prod_toks = []

        def _ldw13(ffb, dram1, dram3, dep_w):
            buf = ffb % 2
            w1tok[buf] = dma(b, "gpsimd",
                             w1b[buf][:].rearrange("p (kt f) -> p kt f", f=512),
                             dram1[:, ffb * 512:(ffb + 1) * 512].rearrange(
                                 "(kt p) f -> p kt f", p=128),
                             deps=dep_w, sem="lw1" + str(buf))
            w3tok[buf] = dma(b, "gpsimd",
                             w3b[buf][:].rearrange("p (kt f) -> p kt f", f=512),
                             dram3[:, ffb * 512:(ffb + 1) * 512].rearrange(
                                 "(kt p) f -> p kt f", p=128),
                             deps=dep_w, sem="lw3" + str(buf))
        if ci == 0:
            _ldw13(0, w1s_in, w3s_in, [b.now("q_tensor")])
        for ffb in range(8):
            buf = ffb % 2
            if ffb + 1 < 8:
                _ldw13(ffb + 1, w1s_in, w3s_in, [b.now("q_tensor")])
            elif ci + 1 < 8:
                pass
            for mt in range(4):
                fi = ffb * 4 + mt
                bank = mt % 2
                tokm = acc_group(b, pm, bank,
                                 [(w1b[buf][:, k * 512 + mt * 128: k * 512 + (mt + 1) * 128],
                                   hch[:, k * 512:(k + 1) * 512]) for k in range(NKT)],
                                 [w1tok[buf], t_h])
                ev = b.op("scalar", lambda q, fi=fi, bank=bank: q.activation(
                    sa[:, fi * 512:(fi + 1) * 512], pm.t[bank][:], AF.Silu),
                    deps=[tokm])
                pm.read(bank, ev)
                bank2 = 2 + mt % 2
                tokm2 = acc_group(b, pm, bank2,
                                  [(w3b[buf][:, k * 512 + mt * 128: k * 512 + (mt + 1) * 128],
                                    hch[:, k * 512:(k + 1) * 512]) for k in range(NKT)],
                                  [w3tok[buf], t_h])
                ev2 = b.op("vector", lambda q, fi=fi, bank2=bank2: q.tensor_copy(
                    sbb[:, fi * 512:(fi + 1) * 512], pm.t[bank2][:]), deps=[tokm2])
                pm.read(bank2, ev2)
                prod_toks.append(b.op("vector", lambda q, fi=fi: q.tensor_tensor(
                    gg[:, fi * 512:(fi + 1) * 512], sa[:, fi * 512:(fi + 1) * 512],
                    sbb[:, fi * 512:(fi + 1) * 512], ALU.mult), deps=[ev, ev2]))
        ysh_toks = []

        def _ldw2(mt, dram2, dep_w):
            buf = mt % 2
            w2tok[buf] = dma(b, "gpsimd",
                             w2b[buf][:].rearrange("p (kt f) -> p kt f", f=128),
                             dram2[:, mt * 128:(mt + 1) * 128].rearrange(
                                 "(kt p) f -> p kt f", p=128),
                             deps=dep_w, sem="lw2" + str(buf))
        _ldw2(0, w2s_in, [b.now("q_tensor")])
        for mt in range(NKT):
            buf = mt % 2
            if mt + 1 < NKT:
                _ldw2(mt + 1, w2s_in, [b.now("q_tensor")])
            bank = 4 + mt % 2
            tokm = acc_group(b, pm, bank,
                             [(w2b[buf][:, k * 128:(k + 1) * 128],
                               gg[:, k * 512:(k + 1) * 512]) for k in range(32)],
                             [w2tok[buf]] + prod_toks)
            ev = b.op("vector", lambda q, mt=mt, bank=bank: q.tensor_copy(
                ysh[:, mt * 512:(mt + 1) * 512], pm.t[bank][:]), deps=[tokm])
            pm.read(bank, ev)
            ysh_toks.append(ev)
        if ci + 1 < 8:
            _ldw13(0, w1s_in, w3s_in, [b.now("q_tensor")])
        sh_store.append(dma(b, "sync",
                            sharedT_out[:, ci * 512:(ci + 1) * 512].rearrange(
                                "(kt p) t -> p kt t", p=128),
                            ysh[:].rearrange("p (kt t) -> p kt t", t=512),
                            deps=ysh_toks, sem="ssh"))

    all_scat = scat_state["toks"][-1]
    g_m = b.op("gpsimd", lambda q: q.indirect_dma_start(
        out=mrows[:], out_offset=None,
        in_=mapd.rearrange("(cc p) one -> cc (p one)", p=128),
        in_offset=bass.IndirectOffsetOnAxis(ap=midx[:], axis=0),
        bounds_check=b.regs["col"], oob_is_err=False),
        deps=[all_scat, t_const, t_mz], sem="gm", dma=True)
    tok_tr = acc_group(b, pm, 6, [(mrows[:], identf[:])], [g_m, t_utf],
                       out=pm.t[6][:, 0:128])
    t_mc = b.op("vector", lambda q: q.tensor_copy(mcolsT[:], pm.t[6][:, 0:128]),
                deps=[tok_tr])
    pm.read(6, t_mc)
    t_mi1 = b.op("vector", lambda q: q.tensor_scalar(
        mrows[:], mcolsT[:], 0.0, None, ALU.is_equal), deps=[t_mc, g_m])
    t_mi2 = b.op("vector", lambda q: q.scalar_tensor_tensor(
        mrows[:], mrows[:], 1e9, mcolsT[:], ALU.mult, ALU.add), deps=[t_mi1])
    t_mi3 = b.op("vector", lambda q: q.tensor_scalar(
        mrows[:], mrows[:], -1.0, None, ALU.add), deps=[t_mi2])
    t_gidx = b.op("vector", lambda q: q.tensor_copy(gidx[:], mrows[:]), deps=[t_mi3])
    sB1.close()
    xe_tok = b.sbuf("xe_tok", [128, 3 * DM], BF16)
    xeT = b.sbuf("xeT", [128, NKT * SEG], BF16)
    ystage = b.sbuf("ystage", [128, 3 * DM], F32)

    y_store = []
    for s in range(7):
        e = s
        seg_dep = [b.now("q_tensor"), t_gidx]
        if s == 0:
            seg_dep += [s_u, s_w, all_scat, g_m]
        t_xz = b.op("vector", lambda q: q.memset(xe_tok[:], 0.0), deps=seg_dep)
        g_toks = []
        for lc in range(3):
            g_toks.append(b.op("gpsimd", lambda q, s=s, lc=lc: q.indirect_dma_start(
                out=xe_tok[:, lc * DM:(lc + 1) * DM], out_offset=None,
                in_=htok_all,
                in_offset=bass.IndirectOffsetOnAxis(
                    ap=gidx[:, s * 3 + lc: s * 3 + lc + 1], axis=0),
                bounds_check=b.regs["tok"], oob_is_err=False),
                deps=[t_gidx, t_xz] + ([b.now("q_vector")] if lc == 0 else []),
                sem="gx", dma=True))
        tr_toks = []
        for lc in range(3):
            for dt in range(NKT):
                bank = 6 + (lc * NKT + dt) % 2
                tokm = acc_group(b, pm, bank,
                                 [(xe_tok[:, lc * DM + dt * 128: lc * DM + (dt + 1) * 128],
                                   identb[:])], g_toks + [t_const],
                                 out=pm.t[bank][:, 0:128])
                ev = b.op("vector", lambda q, lc=lc, dt=dt, bank=bank: q.tensor_copy(
                    xeT[:, dt * SEG + lc * 128: dt * SEG + (lc + 1) * 128],
                    pm.t[bank][:, 0:128]), deps=[tokm])
                pm.read(bank, ev)
                tr_toks.append(ev)
        prod_toks = []

        def _ldw13e(ffb, ee, dep_w):
            buf = ffb % 2
            w1tok[buf] = dma(b, "gpsimd",
                             w1b[buf][:].rearrange("p (kt f) -> p kt f", f=512),
                             w1_in[ee, :, ffb * 512:(ffb + 1) * 512].rearrange(
                                 "(kt p) f -> p kt f", p=128),
                             deps=dep_w, sem="lw1" + str(buf))
            w3tok[buf] = dma(b, "gpsimd",
                             w3b[buf][:].rearrange("p (kt f) -> p kt f", f=512),
                             w3_in[ee, :, ffb * 512:(ffb + 1) * 512].rearrange(
                                 "(kt p) f -> p kt f", p=128),
                             deps=dep_w, sem="lw3" + str(buf))
        _ldw13e(0, e, [b.now("q_tensor")])
        for ffb in range(8):
            buf = ffb % 2
            if ffb + 1 < 8:
                _ldw13e(ffb + 1, e, [b.now("q_tensor")])
            for mt in range(4):
                fi = ffb * 4 + mt
                bank = mt % 2
                tokm = acc_group(b, pm, bank,
                                 [(w1b[buf][:, k * 512 + mt * 128: k * 512 + (mt + 1) * 128],
                                   xeT[:, k * SEG:(k + 1) * SEG]) for k in range(NKT)],
                                 [w1tok[buf]] + tr_toks, out=pm.t[bank][:, 0:SEG])
                ev = b.op("scalar", lambda q, fi=fi, bank=bank: q.activation(
                    sa[:, fi * SEG:(fi + 1) * SEG], pm.t[bank][:, 0:SEG], AF.Silu),
                    deps=[tokm])
                pm.read(bank, ev)
                bank2 = 2 + mt % 2
                tokm2 = acc_group(b, pm, bank2,
                                  [(w3b[buf][:, k * 512 + mt * 128: k * 512 + (mt + 1) * 128],
                                    xeT[:, k * SEG:(k + 1) * SEG]) for k in range(NKT)],
                                  [w3tok[buf]] + tr_toks, out=pm.t[bank2][:, 0:SEG])
                ev2 = b.op("vector", lambda q, fi=fi, bank2=bank2: q.tensor_copy(
                    sbb[:, fi * SEG:(fi + 1) * SEG], pm.t[bank2][:, 0:SEG]),
                    deps=[tokm2])
                pm.read(bank2, ev2)
                prod_toks.append(b.op("vector", lambda q, fi=fi: q.tensor_tensor(
                    gg[:, fi * SEG:(fi + 1) * SEG], sa[:, fi * SEG:(fi + 1) * SEG],
                    sbb[:, fi * SEG:(fi + 1) * SEG], ALU.mult), deps=[ev, ev2]))
        ysb_toks = []

        def _ldw2e(mt, ee, dep_w):
            buf = mt % 2
            w2tok[buf] = dma(b, "gpsimd",
                             w2b[buf][:].rearrange("p (kt f) -> p kt f", f=128),
                             w2_in[ee, :, mt * 128:(mt + 1) * 128].rearrange(
                                 "(kt p) f -> p kt f", p=128),
                             deps=dep_w, sem="lw2" + str(buf))
        _ldw2e(0, e, [b.now("q_tensor")])
        for mt in range(NKT):
            buf = mt % 2
            if mt + 1 < NKT:
                _ldw2e(mt + 1, e, [b.now("q_tensor")])
            bank = 4 + mt % 2
            tokm = acc_group(b, pm, bank,
                             [(w2b[buf][:, k * 128:(k + 1) * 128],
                               gg[:, k * SEG:(k + 1) * SEG]) for k in range(32)],
                             [w2tok[buf]] + prod_toks, out=pm.t[bank][:, 0:SEG])
            ev = b.op("vector", lambda q, mt=mt, bank=bank: q.tensor_copy(
                ysh[:, mt * SEG:(mt + 1) * SEG], pm.t[bank][:, 0:SEG]), deps=[tokm])
            pm.read(bank, ev)
            ysb_toks.append(ev)
        st_toks = []
        for lc in range(3):
            for dt in range(NKT):
                bank = 6 + (lc * NKT + dt) % 2
                tokm = acc_group(b, pm, bank,
                                 [(ysh[:, dt * SEG + lc * 128: dt * SEG + (lc + 1) * 128],
                                   identf[:])], ysb_toks +
                                 (y_store[-1:] if (lc == 0 and dt == 0) else []),
                                 out=pm.t[bank][:, 0:128])
                ev = b.op("vector", lambda q, lc=lc, dt=dt, bank=bank: q.tensor_copy(
                    ystage[:, lc * DM + dt * 128: lc * DM + (dt + 1) * 128],
                    pm.t[bank][:, 0:128]), deps=[tokm])
                pm.read(bank, ev)
                st_toks.append(ev)
        y_store.append(dma(b, "sync", y_out[e * SEG:(e + 1) * SEG, :].rearrange(
            "(lc p) d -> p lc d", p=128),
            ystage[:].rearrange("p (lc d) -> p lc d", d=DM),
            deps=st_toks, sem="sy"))

    b.op("sync", lambda q: q.nop(), deps=b.barrier_tokens())
    b.build_block()
    return nc, es


def prep_B(inputs, resA):
    import ml_dtypes
    identb, identf, utrib = _consts()
    htok_all = np.concatenate([resA[c]["htok"] for c in range(NCORES)], axis=0)
    counts_all = np.stack([resA[c]["counts"][0] for c in range(NCORES)], axis=0)
    IAll = np.zeros((128, 512), np.float32)
    PAll = np.zeros((128, 512), np.float32)
    GAll = np.zeros((128, 512), np.float32)
    for c in range(NCORES):
        r = resA[c]["route"]
        for k in range(2):
            IAll[:, k * 256 + c * 32:(k * 256 + c * 32 + 32)] = r[:, k * 32:(k + 1) * 32]
            PAll[:, k * 256 + c * 32:(k * 256 + c * 32 + 32)] = r[:, (4 + k) * 32:(5 + k) * 32]
            GAll[:, k * 256 + c * 32:(k * 256 + c * 32 + 32)] = r[:, (2 + k) * 32:(3 + k) * 32]
    com = {
        "htok_all": htok_all, "IAll": IAll, "PAll": PAll, "GAll": GAll,
        "counts_all": counts_all.astype(np.float32),
        "w1e": inputs["L0_ffn_W1"], "w3e": inputs["L0_ffn_W3"], "w2e": inputs["L0_ffn_W2"],
        "w1s": inputs["L0_ffn_W1s"], "w3s": inputs["L0_ffn_W3s"], "w2s": inputs["L0_ffn_W2s"],
        "identb": identb, "identf": identf, "utrib": utrib,
    }
    in_maps = []
    for c in range(NCORES):
        mycols = np.full((128, 1), 999, np.uint32)
        for e in range(7):
            for lc in range(3):
                col = e * 23 + (3 * c if c < 7 else 21) + lc
                if c == 7 and lc == 2:
                    col = 999
                if col < 161:
                    mycols[e * 3 + lc, 0] = col
                else:
                    mycols[e * 3 + lc, 0] = 999
        in_maps.append(dict(com, hT=resA[c]["hTb"], mycols=mycols))
    return in_maps



def build_launch_C():
    nc = bass.Bass("TRN2", target_bir_lowering=False)
    es = contextlib.ExitStack()
    b = B(nc, es)
    C = _C()

    h1T = nc.dram_tensor("h1T", [DM, T], F32, kind="ExternalInput").ap()
    sharedT = nc.dram_tensor("sharedT", [DM, T], F32, kind="ExternalInput").ap()
    y_glob = nc.dram_tensor("y_glob", [NSLOT, DM], F32, kind="ExternalInput").ap()
    uwin = nc.dram_tensor("uwin", [128, 128], F32, kind="ExternalInput").ap()
    qkvw = nc.dram_tensor("qkvw", [DM, 1536], F32, kind="ExternalInput").ap()
    identb_in = nc.dram_tensor("identb", [128, 128], BF16, kind="ExternalInput").ap()
    identf_in = nc.dram_tensor("identf", [128, 128], F32, kind="ExternalInput").ap()

    x2T_out = nc.dram_tensor("x2T", [DM, T], F32, kind="ExternalOutput").ap()
    attn2_out = nc.dram_tensor("attn2m", [2048, 256], F32, kind="ExternalOutput").ap()
    mx_out = nc.dram_tensor("mx", [1, 2], F32, kind="ExternalOutput").ap()

    pm = PsumMgr(b)
    identb = b.sbuf("identb_sb", [128, 128], BF16)
    identf = b.sbuf("identf_sb", [128, 128], F32)
    uw = b.sbuf("uw", [128, 128], F32)
    uu = b.sbuf("uu", [128, 64], U32)
    yrows = b.sbuf("yrows", [128, 2 * DM], F32)
    comb = b.sbuf("comb", [128, 4 * DM], F32)
    xf = b.sbuf("xf", [128, NKT * 512], F32)
    shf = b.sbuf("shf", [128, NKT * 512], F32)
    st = b.sbuf("st", [128, NKT * 512], F32)
    C.stb = b.sbuf("stb", [128, NKT * 512], BF16)
    C.sqb = b.sbuf("sqb", [128, NKT * 512], BF16)
    x2b = b.sbuf("x2b", [128, NKT * 512], F32)
    C.rowf = b.sbuf("rowf", [1, 4 * 512], F32)
    C.rowsb = b.sbuf("rowsb", [1, 2 * 512], BF16)
    C.onesc = b.sbuf("onesc", [128, 1], BF16)
    C.onesr = b.sbuf("onesr", [1, 128], BF16)
    onescf = b.sbuf("onescf_c", [128, 1], F32)
    xmb = b.sbuf("xmb", [128, NKT * 256], BF16)
    xmf = b.sbuf("xmf", [128, NKT * 256], F32)
    qkv_sb = b.sbuf("qkv_sb", [128, NKT * 512], BF16)
    qlT = b.sbuf("qlT", [128, 4 * 256], BF16)
    klT = b.sbuf("klT", [128, 4 * 256], BF16)
    a2 = b.sbuf("a2", [128, 2 * 256], F32)
    a2r = b.sbuf("a2r", [128, 8], F32)
    a2c = b.sbuf("a2c", [1, 2 * 256], F32)
    mxr = b.sbuf("mxr", [1, 64], F32)

    t_id = dma(b, "sync", identb[:], identb_in, sem="ldc")
    dma(b, "sync", identf[:], identf_in, sem="ldc")
    t_uw = dma(b, "sync", uw[:], uwin, sem="ldc")
    t_const = t_uw
    t_ones = b.op("vector", lambda q: q.memset(C.onesc[:], 1.0))
    t_onesr = b.op("vector", lambda q: q.memset(C.onesr[:], 1.0))
    t_onescf = b.op("vector", lambda q: q.memset(onescf[:], 1.0))
    C.t_ones = t_ones
    C.t_onesr = t_onesr
    b.regs = {}

    def _init_regs(q):
        r1 = q.alloc_register()
        b.regs["slot"] = r1
        return q.reg_mov(r1, NSLOT - 1)
    t_regs = b.op("gpsimd", _init_regs)
    t_uu = b.op("vector", lambda q: q.tensor_copy(uu[:], uw[:, 0:64]), deps=[t_uw])

    prev = []
    for ch in range(8):
        cdep = [b.now("q_tensor"), b.now("q_vector"), b.now("q_scalar")] + prev
        t_h1 = dma(b, "sync", xf[:].rearrange("p (kt t) -> p kt t", t=512),
                   h1T[:, ch * 512:(ch + 1) * 512].rearrange("(kt p) t -> p kt t", p=128),
                   deps=cdep, sem="ld1")
        t_sh = dma(b, "sync", shf[:].rearrange("p (kt t) -> p kt t", t=512),
                   sharedT[:, ch * 512:(ch + 1) * 512].rearrange("(kt p) t -> p kt t", p=128),
                   deps=cdep, sem="ld1")
        comb_toks = []
        for tt4 in range(4):
            tile = ch * 4 + tt4
            g1 = b.op("gpsimd", lambda q, tile=tile: q.indirect_dma_start(
                out=yrows[:, 0:DM], out_offset=None, in_=y_glob,
                in_offset=bass.IndirectOffsetOnAxis(ap=uu[:, tile:tile + 1], axis=0),
                bounds_check=b.regs["slot"], oob_is_err=False),
                deps=[t_uu, t_regs] + cdep + (comb_toks[-1:] if comb_toks else []),
                sem="gy", dma=True)
            g2 = b.op("gpsimd", lambda q, tile=tile: q.indirect_dma_start(
                out=yrows[:, DM:2 * DM], out_offset=None, in_=y_glob,
                in_offset=bass.IndirectOffsetOnAxis(ap=uu[:, 32 + tile:32 + tile + 1], axis=0),
                bounds_check=b.regs["slot"], oob_is_err=False),
                deps=[], sem="gy", dma=True)
            c1 = b.op("vector", lambda q, tile=tile, tt4=tt4: q.tensor_scalar(
                comb[:, tt4 * DM:(tt4 + 1) * DM], yrows[:, 0:DM],
                uw[:, 64 + tile:64 + tile + 1], None, ALU.mult), deps=[g2])
            comb_toks.append(b.op("vector", lambda q, tile=tile, tt4=tt4:
                q.scalar_tensor_tensor(
                    comb[:, tt4 * DM:(tt4 + 1) * DM], yrows[:, DM:2 * DM],
                    uw[:, 96 + tile:96 + tile + 1],
                    comb[:, tt4 * DM:(tt4 + 1) * DM], ALU.mult, ALU.add), deps=[c1]))
        st_toks = []
        for ft in range(NKT):
            st_toks.append(b.op("vector", lambda q, ft=ft: q.tensor_tensor(
                st[:, ft * 512:(ft + 1) * 512], xf[:, ft * 512:(ft + 1) * 512],
                shf[:, ft * 512:(ft + 1) * 512], ALU.add), deps=[t_h1, t_sh]))
        tr_toks = []
        for tt4 in range(4):
            for ft in range(NKT):
                bank = (tt4 * NKT + ft) % 2
                tokm = acc_group(b, pm, bank,
                                 [(comb[:, tt4 * DM + ft * 128: tt4 * DM + (ft + 1) * 128],
                                   identf[:])], comb_toks + [t_const],
                                 out=pm.t[bank][:, 0:128])
                ev = b.op("vector", lambda q, tt4=tt4, ft=ft, bank=bank: q.tensor_tensor(
                    st[:, ft * 512 + tt4 * 128: ft * 512 + (tt4 + 1) * 128],
                    st[:, ft * 512 + tt4 * 128: ft * 512 + (tt4 + 1) * 128],
                    pm.t[bank][:, 0:128], ALU.add),
                    deps=[tokm, st_toks[ft]])
                pm.read(bank, ev)
                tr_toks.append(ev)
        bm, br = ln_stats(b, pm, C, st[:], tr_toks, 7, 4, 5)
        x2_toks = []
        for ft in range(NKT):
            a1 = b.op("vector", lambda q, ft=ft: q.tensor_tensor(
                x2b[:, ft * 512:(ft + 1) * 512], st[:, ft * 512:(ft + 1) * 512],
                pm.t[4][:], ALU.subtract), deps=[bm] + tr_toks)
            a2t = b.op("vector", lambda q, ft=ft: q.tensor_tensor(
                x2b[:, ft * 512:(ft + 1) * 512], x2b[:, ft * 512:(ft + 1) * 512],
                pm.t[5][:], ALU.mult), deps=[br, a1])
            pm.read(4, a1)
            pm.read(5, a2t)
            x2_toks.append(a2t)
        s_x2 = dma(b, "sync", x2T_out[:, ch * 512:(ch + 1) * 512].rearrange(
            "(kt p) t -> p kt t", p=128),
            x2b[:].rearrange("p (kt t) -> p kt t", t=512), deps=x2_toks, sem="sx2")
        lm_toks = []
        for ft in range(NKT):
            lm_toks.append(b.op("vector", lambda q, ft=ft, ch=ch: q.tensor_reduce(
                v3(xmf, (128, 32, 1), (2048, 1, 1), off=ft * 256 + ch * 32),
                v3(x2b, (128, 32, 16), (4096, 16, 1), off=ft * 512),
                axis=AX.X, op=ALU.add), deps=x2_toks))
        prev = [s_x2] + lm_toks
    t_xm = b.op("vector", lambda q: q.tensor_scalar(
        xmb[:], xmf[:], 1.0 / 16.0, None, ALU.mult), deps=prev)
    t_qw = dma(b, "gpsimd", qkv_sb[:].rearrange("p (kt f) -> p kt f", f=512),
               qkvw[:, 0:512].rearrange("(kt p) f -> p kt f", p=128), sem="lq")
    ql_toks = []
    for mt in range(4):
        bank = mt % 2
        tokm = acc_group(b, pm, bank,
                         [(qkv_sb[:, k * 512 + mt * 128: k * 512 + (mt + 1) * 128],
                           xmb[:, k * 256:(k + 1) * 256]) for k in range(NKT)],
                         [t_qw, t_xm], out=pm.t[bank][:, 0:256])
        ev = b.op("scalar", lambda q, mt=mt, bank=bank: q.activation(
            qlT[:, mt * 256:(mt + 1) * 256], pm.t[bank][:, 0:256], AF.Copy,
            scale=1.0 / 8.0), deps=[tokm])
        pm.read(bank, ev)
        ql_toks.append(ev)
    t_kw = dma(b, "gpsimd", qkv_sb[:].rearrange("p (kt f) -> p kt f", f=512),
               qkvw[:, 512:1024].rearrange("(kt p) f -> p kt f", p=128),
               deps=[b.now("q_tensor")], sem="lq")
    kl_toks = []
    for mt in range(4):
        bank = mt % 2
        tokm = acc_group(b, pm, bank,
                         [(qkv_sb[:, k * 512 + mt * 128: k * 512 + (mt + 1) * 128],
                           xmb[:, k * 256:(k + 1) * 256]) for k in range(NKT)],
                         [t_kw, t_xm], out=pm.t[bank][:, 0:256])
        ev = b.op("vector", lambda q, mt=mt, bank=bank: q.tensor_copy(
            klT[:, mt * 256:(mt + 1) * 256], pm.t[bank][:, 0:256]), deps=[tokm])
        pm.read(bank, ev)
        kl_toks.append(ev)
    mx_toks = []
    st_toks2 = []
    prev_sa2 = None
    for h in range(8):
        pt = h // 2
        pb = (h % 2) * 64
        for mt in range(2):
            bank = 2 + mt
            tokm = acc_group(b, pm, bank,
                             [(qlT[pb:pb + 64, pt * 256 + mt * 128: pt * 256 + (mt + 1) * 128],
                               klT[pb:pb + 64, pt * 256:(pt + 1) * 256])],
                             ql_toks + kl_toks, out=pm.t[bank][:, 0:256])
            t_mx = b.op("vector", lambda q, bank=bank: q.reduce_max(
                a2r[:, 0:1], pm.t[bank][:, 0:256], axis=AX.X), deps=[tokm])
            t_ng = b.op("vector", lambda q: q.tensor_scalar(
                a2r[:, 1:2], a2r[:, 0:1], -1.0, None, ALU.mult), deps=[t_mx])
            t_ex = b.op("scalar", lambda q, mt=mt, bank=bank: q.activation(
                a2[:, mt * 256:(mt + 1) * 256], pm.t[bank][:, 0:256], AF.Exp,
                bias=a2r[:, 1:2]),
                deps=[t_ng] + ([prev_sa2] if prev_sa2 else []))
            pm.read(bank, t_ex)
            t_sm = b.op("vector", lambda q, mt=mt: q.reduce_sum(
                a2r[:, 2:3], a2[:, mt * 256:(mt + 1) * 256], axis=AX.X), deps=[t_ex])
            t_rc = b.op("vector", lambda q: q.reciprocal(a2r[:, 3:4], a2r[:, 2:3]),
                        deps=[t_sm])
            t_nm = b.op("scalar", lambda q, mt=mt: q.activation(
                a2[:, mt * 256:(mt + 1) * 256], a2[:, mt * 256:(mt + 1) * 256],
                AF.Copy, scale=a2r[:, 3:4]), deps=[t_rc])
            st_toks2.append(t_nm)
        s_a2 = dma(b, "sync", attn2_out[h * 256:(h + 1) * 256, :].rearrange(
            "(mt p) j -> p mt j", p=128),
            a2[:].rearrange("p (mt j) -> p mt j", j=256), deps=st_toks2[-2:], sem="sa2")
        prev_sa2 = s_a2
        for mt in range(2):
            t_rs = b.op("vector", lambda q, mt=mt, h=h: q.tensor_reduce(
                a2r[:, 4 + 0:5 + 0], a2[:, mt * 256:(mt + 1) * 256], axis=AX.X,
                op=ALU.add), deps=[st_toks2[-2 + mt]])
            tokc = acc_group(b, pm, 6, [(C.onesc[:], C.stb[0:128, 0:1])],
                             [t_rs], out=pm.t[6][0:1, 0:1]) if False else None
            tokt = acc_group(b, pm, 6, [(a2r[:, 4:5], identf[:])],
                             [t_rs], out=pm.t[6][0:1, 0:128])
            t_cp = b.op("vector", lambda q, h=h, mt=mt: q.reduce_max(
                mxr[0:1, h * 2 + mt: h * 2 + mt + 1], pm.t[6][0:1, 0:128],
                axis=AX.X), deps=[tokt])
            pm.read(6, t_cp)
            mx_toks.append(t_cp)
            tokcs = acc_group(b, pm, 6, [(onescf[:], a2[:, mt * 256:(mt + 1) * 256])],
                              [st_toks2[-2 + mt], t_onescf], out=pm.t[6][0:1, 0:256])
            t_c2 = b.op("vector", lambda q, h=h, mt=mt: q.tensor_copy(
                a2c[0:1, mt * 256:(mt + 1) * 256], pm.t[6][0:1, 0:256]), deps=[tokcs])
            pm.read(6, t_c2)
            mx_toks.append(t_c2)
        t_cc = b.op("vector", lambda q, h=h: q.tensor_tensor(
            a2c[0:1, 0:256], a2c[0:1, 0:256], a2c[0:1, 256:512], ALU.add),
            deps=mx_toks[-2:])
        t_cm = b.op("vector", lambda q, h=h: q.reduce_max(
            mxr[0:1, 32 + h:33 + h], a2c[0:1, 0:256], axis=AX.X), deps=[t_cc])
        mx_toks.append(t_cm)
    t_f1 = b.op("vector", lambda q: q.reduce_max(mxr[0:1, 40:41], mxr[0:1, 0:16],
                                                 axis=AX.X), deps=mx_toks)
    t_f2 = b.op("vector", lambda q: q.reduce_max(mxr[0:1, 41:42], mxr[0:1, 32:40],
                                                 axis=AX.X), deps=mx_toks)
    s_mx = dma(b, "sync", mx_out, mxr[0:1, 40:42], deps=[t_f1, t_f2], sem="smx")
    b.op("sync", lambda q: q.nop(), deps=b.barrier_tokens())
    b.build_block()
    return nc, es


def prep_C(inputs, resA, resB):
    identb, identf, utrib = _consts()
    y_glob = np.zeros((NSLOT, DM), np.float32)
    for c in range(NCORES):
        y = resB[c]["y_out"]
        ncols = 3 if c < 7 else 2
        for e in range(7):
            base = e * CAPP + (3 * c if c < 7 else 21) * 128
            y_glob[base:base + ncols * 128] = y[e * SEG: e * SEG + ncols * 128]
    u_all = resB[0]["u_all"]
    w_all = resB[0]["w_all"]
    in_maps = []
    for c in range(NCORES):
        uwc = np.zeros((128, 128), np.float32)
        uwc[:, 0:32] = u_all[:, c * 32:(c + 1) * 32]
        uwc[:, 32:64] = u_all[:, 256 + c * 32: 256 + (c + 1) * 32]
        uwc[:, 64:96] = w_all[:, c * 32:(c + 1) * 32]
        uwc[:, 96:128] = w_all[:, 256 + c * 32: 256 + (c + 1) * 32]
        in_maps.append({
            "h1T": resA[c]["h1T"], "sharedT": resB[c]["sharedT"],
            "y_glob": y_glob, "uwin": uwc,
            "qkvw": inputs["L1_attn_qkv_w"],
            "identb": identb, "identf": identf,
        })
    return in_maps




def build_launch_D():
    nc = bass.Bass("TRN2", target_bir_lowering=False)
    es = contextlib.ExitStack()
    b = B(nc, es)

    x2T = nc.dram_tensor("x2T", [DM, T], F32, kind="ExternalInput").ap()
    x20T = nc.dram_tensor("x20T", [DM, T], F32, kind="ExternalInput").ap()
    attn20 = nc.dram_tensor("attn20", [2048, 256], F32, kind="ExternalInput").ap()
    mxall = nc.dram_tensor("mxall", [1, 16], F32, kind="ExternalInput").ap()
    qkvw = nc.dram_tensor("qkvw", [DM, 1536], F32, kind="ExternalInput").ap()
    outw = nc.dram_tensor("outw", [512, DM], F32, kind="ExternalInput").ap()
    outb = nc.dram_tensor("outb", [128, NKT], F32, kind="ExternalInput").ap()
    wqin = nc.dram_tensor("wqin", [DM, 256], F32, kind="ExternalInput").ap()
    wkin = nc.dram_tensor("wkin", [DM, 256], F32, kind="ExternalInput").ap()
    wvw = nc.dram_tensor("wvw", [DM, DM], F32, kind="ExternalInput").ap()
    wvb = nc.dram_tensor("wvb", [128, NKT], F32, kind="ExternalInput").ap()
    bands = nc.dram_tensor("bands", [8 * 160, 128], F32, kind="ExternalInput").ap()
    erows = nc.dram_tensor("erows", [128, 128], I32, kind="ExternalInput").ap()
    ecols = nc.dram_tensor("ecols", [128, 128], I32, kind="ExternalInput").ap()
    evals = nc.dram_tensor("evals", [128, 128], F32, kind="ExternalInput").ap()
    identb_in = nc.dram_tensor("identb", [128, 128], BF16, kind="ExternalInput").ap()
    identf_in = nc.dram_tensor("identf", [128, 128], F32, kind="ExternalInput").ap()

    A_out = nc.dram_tensor("A_c", [128, 32], F32, kind="ExternalOutput").ap()
    xg0T_out = nc.dram_tensor("xg0T", [DM, T], F32, kind="ExternalOutput").ap()
    valT_out = nc.dram_tensor("valT", [DM, T], F32, kind="ExternalOutput").ap()

    v0d = nc.dram_tensor("v0d", [T, 64], BF16, kind="Internal").ap()
    qed = nc.dram_tensor("qed", [T, 256], BF16, kind="Internal").ap()
    ked = nc.dram_tensor("ked", [T, 256], BF16, kind="Internal").ap()

    pm = PsumMgr(b)
    identb = b.sbuf("identb_sb", [128, 128], BF16)
    identf = b.sbuf("identf_sb", [128, 128], F32)
    rowsc = b.sbuf("rowsc", [1, 64], F32)
    onesr = b.sbuf("onesr", [1, 128], BF16)
    t2p = b.sbuf("t2p", [128, 8 * 2 * 65], BF16)
    zres = b.sbuf("zres", [128, 16 * 256], BF16)
    ztres = b.sbuf("ztres", [128, 16 * 256], BF16)
    t_id = dma(b, "sync", identb[:], identb_in, sem="ldc")
    t_if = dma(b, "sync", identf[:], identf_in, sem="ldc")
    t_mx = dma(b, "sync", rowsc[0:1, 0:16], mxall, sem="ldc")
    t_const = t_mx
    t_id = t_if = t_const
    t_or = b.op("vector", lambda q: q.memset(onesr[:], 1.0))

    t_s1 = b.op("vector", lambda q: q.reduce_max(rowsc[0:1, 16:17], rowsc[0:1, 0:8],
                                                 axis=AX.X), deps=[t_const])
    t_s2 = b.op("vector", lambda q: q.reduce_max(rowsc[0:1, 17:18], rowsc[0:1, 8:16],
                                                 axis=AX.X), deps=[t_const])
    t_s3 = b.op("vector", lambda q: q.tensor_tensor(
        rowsc[0:1, 18:19], rowsc[0:1, 16:17], rowsc[0:1, 17:18], ALU.mult),
        deps=[t_s1, t_s2])
    t_s4 = b.op("vector", lambda q: q.reciprocal(rowsc[0:1, 19:20], rowsc[0:1, 18:19]),
                deps=[t_s3])
    t_s4b = b.op("vector", lambda q: q.tensor_copy(rowsc[0:1, 20:21], rowsc[0:1, 19:20]),
                 deps=[t_s4])
    rsc = b.sbuf("rsc", [128, 1], F32)
    onesrf = b.sbuf("onesrf", [1, 128], F32)
    t_orf = b.op("vector", lambda q: q.memset(onesrf[:], 1.0))
    tokb = acc_group(b, pm, 6, [(onesrf[:], rowsc[0:1, 19:20])], [t_s4, t_orf],
                     out=pm.t[6][:, 0:1])
    t_rsc = b.op("vector", lambda q: q.tensor_copy(rsc[:], pm.t[6][:, 0:1]), deps=[tokb])
    pm.read(6, t_rsc)

    s0 = contextlib.ExitStack()
    a2f = b.sbuf("a2f", [128, 16 * 256], F32, stack=s0)
    a2b = b.sbuf("a2b", [128, 16 * 256], BF16, stack=s0)
    a2t = b.sbuf("a2t", [128, 16 * 256], BF16, stack=s0)
    zb = b.sbuf("zb", [128, 16 * 256], BF16, stack=s0)
    zt = b.sbuf("zt", [128, 16 * 256], BF16, stack=s0)
    xzb = b.sbuf("xzb", [128, 16 * 256], BF16, stack=s0)
    xzt = b.sbuf("xzt", [128, 16 * 256], BF16, stack=s0)
    wb_ = b.sbuf("wb_", [128, 16 * 256], BF16, stack=s0)

    t_a2 = dma(b, "sync", a2f[:].rearrange("p (m j) -> p m j", j=256),
               attn20.rearrange("(m p) j -> p m j", p=128), sem="lda")
    t_a2b = b.op("vector", lambda q: q.tensor_copy(a2b[:], a2f[:]), deps=[t_a2])

    def sq_off(h, mt):
        return (h * 2 + mt) * 256

    tr_toks = []
    for h in range(8):
        for mt in range(2):
            for jt in range(2):
                bank = (len(tr_toks)) % 2
                tokm = acc_group(b, pm, bank,
                                 [(a2b[:, sq_off(h, mt) + jt * 128: sq_off(h, mt) + (jt + 1) * 128],
                                   identb[:])], [t_a2b, t_id], out=pm.t[bank][:, 0:128])
                ev = b.op("vector", lambda q, h=h, mt=mt, jt=jt, bank=bank: q.tensor_copy(
                    a2t[:, sq_off(h, jt) + mt * 128: sq_off(h, jt) + (mt + 1) * 128],
                    pm.t[bank][:, 0:128]), deps=[tokm])
                pm.read(bank, ev)
                tr_toks.append(ev)
    t_z0 = b.op("scalar", lambda q: q.activation(zb[:], a2t[:], AF.Copy,
                                                 scale=rsc[:, 0:1]),
                deps=tr_toks + [t_rsc])
    zdep = {h: t_z0 for h in range(8)}

    def matmul_sq(dst, dstoff, lhsT_buf, loff, rhs_buf, roff, deps, n=256,
                  evac="copy", extra=None):
        toks = []
        for mt in range(2):
            bank = 4 + mt
            g = acc_group(b, pm, bank,
                          [(lhsT_buf[:, loff + kt * 128 + 0:0] if False else
                            lhsT_buf[:, loff + kt * 256 + mt * 128: loff + kt * 256 + (mt + 1) * 128],
                            rhs_buf[:, roff + kt * 256: roff + kt * 256 + n])
                           for kt in range(2)],
                          deps, out=pm.t[bank][:, 0:n])
            if evac == "copy":
                ev = b.op("vector", lambda q, mt=mt, bank=bank: q.tensor_copy(
                    dst[:, dstoff + mt * 256: dstoff + mt * 256 + n],
                    pm.t[bank][:, 0:n]), deps=[g])
            else:
                cval, diag = extra
                if diag:
                    ev = b.op("vector", lambda q, mt=mt, bank=bank, cval=cval:
                        q.scalar_tensor_tensor(
                            dst[:, dstoff + mt * 256 + mt * 128: dstoff + mt * 256 + mt * 128 + 128]
                            if False else dst[:, dstoff + mt * 256: dstoff + mt * 256 + n],
                            identf[:], cval, pm.t[bank][:, 0:n], ALU.mult, ALU.subtract),
                        deps=[g])
                else:
                    ev = None
            pm.read(bank, ev)
            toks.append(ev)
        return toks

    def emit_w(dst, h, cval, src_xz, deps):
        toks = []
        for mt in range(2):
            tneg = b.op("vector", lambda q, mt=mt: q.tensor_scalar(
                dst[:, sq_off(h, mt): sq_off(h, mt) + 256],
                src_xz[:, sq_off(h, mt): sq_off(h, mt) + 256], -1.0, None, ALU.mult),
                deps=deps)
            tdi = b.op("vector", lambda q, mt=mt, cval=cval: q.scalar_tensor_tensor(
                dst[:, sq_off(h, mt) + mt * 128: sq_off(h, mt) + mt * 128 + 128],
                identb[:], cval,
                dst[:, sq_off(h, mt) + mt * 128: sq_off(h, mt) + mt * 128 + 128],
                ALU.mult, ALU.add), deps=[tneg, t_id])
            toks.append(tdi)
        return toks

    def transpose_sq(dst, src, h, deps):
        toks = []
        for mt in range(2):
            for jt in range(2):
                bank = (mt * 2 + jt) % 2
                tokm = acc_group(b, pm, bank,
                                 [(src[:, sq_off(h, mt) + jt * 128: sq_off(h, mt) + (jt + 1) * 128],
                                   identb[:])], deps + [t_id], out=pm.t[bank][:, 0:128])
                ev = b.op("vector", lambda q, h=h, mt=mt, jt=jt, bank=bank: q.tensor_copy(
                    dst[:, sq_off(h, jt) + mt * 128: sq_off(h, jt) + (mt + 1) * 128],
                    pm.t[bank][:, 0:128]), deps=[tokm])
                pm.read(bank, ev)
                toks.append(ev)
        return toks

    for it in range(6):
        xz_toks = {}
        for h in range(8):
            toks = matmul_sq(xzb, sq_off(h, 0), a2t, sq_off(h, 0), zb, sq_off(h, 0),
                             [zdep[h]] + (tr_toks if it == 0 else []))
            xz_toks[h] = toks
        xzt_toks = {}
        for h in range(8):
            xzt_toks[h] = transpose_sq(xzt, xzb, h, xz_toks[h])
        w_toks = {}
        for h in range(8):
            w_toks[h] = emit_w(wb_, h, 7.0, xzb, xz_toks[h])
        p2_toks = {}
        for h in range(8):
            p2_toks[h] = matmul_sq(zt, sq_off(h, 0), xzt, sq_off(h, 0), wb_, sq_off(h, 0),
                                   w_toks[h] + xzt_toks[h])
        w2_toks = {}
        for h in range(8):
            w2_toks[h] = emit_w(wb_, h, 15.0, zt, p2_toks[h])
        p3_toks = {}
        for h in range(8):
            p3_toks[h] = matmul_sq(zt, sq_off(h, 0), xzt, sq_off(h, 0), wb_, sq_off(h, 0),
                                   w2_toks[h])
        w3_toks = {}
        for h in range(8):
            w3_toks[h] = emit_w(wb_, h, 13.0, zt, p3_toks[h])
        zt_toks = {}
        for h in range(8):
            zt_toks[h] = transpose_sq(xzt, zb, h, [zdep[h]])
        for h in range(8):
            toks = []
            for mt in range(2):
                bank = 4 + mt
                g = acc_group(b, pm, bank,
                              [(xzt[:, sq_off(h, kt) + mt * 128: sq_off(h, kt) + (mt + 1) * 128],
                                wb_[:, sq_off(h, kt): sq_off(h, kt) + 256])
                               for kt in range(2)],
                              w3_toks[h] + zt_toks[h], out=pm.t[bank][:, 0:256])
                ev = b.op("scalar", lambda q, h=h, mt=mt, bank=bank: q.activation(
                    zb[:, sq_off(h, mt): sq_off(h, mt) + 256], pm.t[bank][:, 0:256],
                    AF.Copy, scale=0.25), deps=[g])
                pm.read(bank, ev)
                toks.append(ev)
            zdep[h] = toks[-1]
    t_zres = b.op("vector", lambda q: q.tensor_copy(zres[:], zb[:]),
                  deps=[zdep[h] for h in range(8)])
    zt_final = []
    for h in range(8):
        zt_final += transpose_sq(ztres, zb, h, [zdep[h]])
    ph0_done = [b.now("q_tensor"), b.now("q_vector"), b.now("q_scalar"), t_zres]
    s0.close()

    s1 = contextlib.ExitStack()
    x20b = b.sbuf("x20b", [128, NKT * T], BF16, stack=s1)
    xm0f = b.sbuf("xm0f", [128, NKT * 256], F32, stack=s1)
    xm0b = b.sbuf("xm0b", [128, NKT * 256], BF16, stack=s1)
    wsl = b.sbuf("wsl", [128, NKT * 64], BF16, stack=s1)
    wsl2 = b.sbuf("wsl2", [128, NKT * 64], BF16, stack=s1)
    wsl3 = b.sbuf("wsl3", [128, NKT * 64], BF16, stack=s1)
    ql0 = b.sbuf("ql0", [64, 256], BF16, stack=s1)
    kl0 = b.sbuf("kl0", [64, 256], BF16, stack=s1)
    q0b = b.sbuf("q0b", [64, T], BF16, stack=s1)
    k0b = b.sbuf("k0b", [64, T], BF16, stack=s1)
    v0p = b.sbuf("v0p", [128, 32 * 65], BF16, stack=s1)
    v0pad = b.sbuf("v0pad", [128, 33 * 64], BF16, stack=s1)
    es3 = b.sbuf("es3", [128, 32 * 256], BF16, stack=s1)
    p1t = b.sbuf("p1t", [128, 2 * T], BF16, stack=s1)
    convt = b.sbuf("convt", [128, 32 * 64], BF16, stack=s1)
    band0 = b.sbuf("band0", [128, 128], BF16, stack=s1)
    band1 = b.sbuf("band1", [32, 128], BF16, stack=s1)
    outall = b.sbuf("outall", [128, 32 * 512], BF16, stack=s1)
    t1f = b.sbuf("t1f", [128, 256], F32, stack=s1)
    t1b = b.sbuf("t1b", [128, 16 * 64], BF16, stack=s1)

    t_x20 = [dma(b, "gpsimd", x20b[:, k * T:(k + 1) * T], x20T[k * 128:(k + 1) * 128, :],
                 deps=ph0_done, sem="ldx0") for k in range(NKT)]
    lm_toks = []
    for ft in range(NKT):
        lm_toks.append(b.op("vector", lambda q, ft=ft: q.tensor_reduce(
            v3(xm0f, (128, 256, 1), (2048, 1, 1), off=ft * 256),
            v3(x20b, (128, 256, 16), (NKT * T, 16, 1), off=ft * T),
            axis=AX.X, op=ALU.add), deps=[t_x20[-1]]))
    t_xm0 = b.op("vector", lambda q: q.tensor_scalar(
        xm0b[:], xm0f[:], 1.0 / 16.0, None, ALU.mult), deps=lm_toks)

    head_toks = []
    for h in range(8):
        hd = [b.now("q_tensor"), b.now("q_scalar"), b.now("q_vector")]
        t_wq = dma(b, "gpsimd", wsl[:].rearrange("p (kt f) -> p kt f", f=64),
                   qkvw[:, h * 64:(h + 1) * 64].rearrange("(kt p) f -> p kt f", p=128),
                   deps=hd, sem="lwh")
        t_wk = dma(b, "gpsimd", wsl2[:].rearrange("p (kt f) -> p kt f", f=64),
                   qkvw[:, 512 + h * 64: 512 + (h + 1) * 64].rearrange(
                       "(kt p) f -> p kt f", p=128), deps=hd, sem="lwh")
        t_wv = dma(b, "gpsimd", wsl3[:].rearrange("p (kt f) -> p kt f", f=64),
                   qkvw[:, 1024 + h * 64: 1024 + (h + 1) * 64].rearrange(
                       "(kt p) f -> p kt f", p=128), deps=hd, sem="lwh")
        t_wq = t_wk = t_wv
        g1 = acc_group(b, pm, 0, [(wsl[:, k * 64:(k + 1) * 64], xm0b[:, k * 256:(k + 1) * 256])
                                  for k in range(NKT)], [t_wq, t_xm0] + hd,
                       out=pm.t[0][0:64, 0:256])
        e1 = b.op("scalar", lambda q: q.activation(ql0[:], pm.t[0][0:64, 0:256],
                                                   AF.Copy, scale=1.0 / 8.0), deps=[g1])
        pm.read(0, e1)
        g2 = acc_group(b, pm, 1, [(wsl2[:, k * 64:(k + 1) * 64], xm0b[:, k * 256:(k + 1) * 256])
                                  for k in range(NKT)], [t_wk, t_xm0], out=pm.t[1][0:64, 0:256])
        e2 = b.op("vector", lambda q: q.tensor_copy(kl0[:], pm.t[1][0:64, 0:256]), deps=[g2])
        pm.read(1, e2)
        qk_toks = []
        for ch in range(8):
            g3 = acc_group(b, pm, 0, [(wsl[:, k * 64:(k + 1) * 64],
                                       x20b[:, k * T + ch * 512: k * T + (ch + 1) * 512])
                                      for k in range(NKT)], [e1], out=pm.t[0][0:64, :])
            e3 = b.op("scalar", lambda q, ch=ch: q.activation(
                q0b[:, ch * 512:(ch + 1) * 512], pm.t[0][0:64, :], AF.Copy,
                scale=1.0 / 8.0), deps=[g3])
            pm.read(0, e3)
            g4 = acc_group(b, pm, 1, [(wsl2[:, k * 64:(k + 1) * 64],
                                       x20b[:, k * T + ch * 512: k * T + (ch + 1) * 512])
                                      for k in range(NKT)], [e2], out=pm.t[1][0:64, :])
            e4 = b.op("vector", lambda q, ch=ch: q.tensor_copy(
                k0b[:, ch * 512:(ch + 1) * 512], pm.t[1][0:64, :]), deps=[g4])
            pm.read(1, e4)
            qk_toks += [e3, e4]
        t_vz = b.op("vector", lambda q: q.memset(v0pad[:], 0.0), deps=hd)
        t_vo = b.op("vector", lambda q: q.memset(
            v0p[:].rearrange("p (tt c) -> p tt c", c=65)[:, :, 64:65], 1.0), deps=hd)
        v_toks = []
        for tt in range(32):
            bank = 2 + tt % 2
            g5 = acc_group(b, pm, bank,
                           [(x20b[:, k * T + tt * 128:k * T + (tt + 1) * 128],
                             wsl3[:, k * 64:(k + 1) * 64]) for k in range(NKT)],
                           [t_wv, t_vo] + (hd if tt < 2 else []), out=pm.t[bank][:, 0:64])
            e5 = b.op("vector", lambda q, tt=tt, bank=bank: q.tensor_copy(
                v0p[:, tt * 65: tt * 65 + 64], pm.t[bank][:, 0:64]), deps=[g5])
            pm.read(bank, e5)
            v_toks.append(e5)
        s_v0 = dma(b, "sync", v0d.rearrange("(tt p) d -> p tt d", p=128),
                   v0p[:].rearrange("p (tt c) -> p tt c", c=65)[:, :, 0:64],
                   deps=v_toks, sem="sv0")
        l_vp = dma(b, "sync",
                   v3(v0pad, (128, 31, 64), (33 * 64, 64, 1), off=64),
                   v0d[112:4080, :].rearrange("(j p) d -> p j d", p=128),
                   deps=[s_v0, t_vz], sem="lvp")
        l_vpb = dma(b, "sync", v0pad[0:16, 32 * 64:33 * 64],
                    v0d[4080:4096, :], deps=[s_v0, t_vz], sem="lvp")
        l_vp0 = dma(b, "sync", v0pad[16:128, 0:64],
                    v0d[0:112, :], deps=[s_v0, t_vz], sem="lvp")
        t_b0 = dma(b, "gpsimd", band0[:], bands[h * 160: h * 160 + 128, :],
                   deps=hd, sem="lbd")
        t_b1 = dma(b, "gpsimd", band1[:], bands[h * 160 + 128: h * 160 + 160, :],
                   deps=hd, sem="lbd")
        cv_toks = []
        for tt in range(32):
            bank = 2 + tt % 2
            deps0 = pm.war(bank) + [t_b0, t_b1, l_vp, l_vpb, l_vp0]
            mm(b, pm.t[bank][:, 0:64], band0[:],
               v0pad[:, tt * 64:(tt + 1) * 64], start=True, stop=False, deps=deps0)
            g6 = mm(b, pm.t[bank][:, 0:64], band1[:],
                    v0pad[0:32, (tt + 1) * 64:(tt + 2) * 64], start=False, stop=True,
                    deps=[])
            pm.wrote(bank)
            e6 = b.op("vector", lambda q, tt=tt, bank=bank: q.tensor_copy(
                convt[:, tt * 64:(tt + 1) * 64], pm.t[bank][:, 0:64]), deps=[g6])
            pm.read(bank, e6)
            cv_toks.append(e6)
        s3_toks = []
        for jt in range(32):
            bank = 2 + jt % 2
            g7 = acc_group(b, pm, bank,
                           [(k0b[:, jt * 128:(jt + 1) * 128], ql0[0:64, 0:256])],
                           qk_toks if jt < 2 else [], out=pm.t[bank][:, 0:256])
            e7 = b.op("scalar", lambda q, jt=jt, bank=bank: q.activation(
                es3[:, jt * 256:(jt + 1) * 256], pm.t[bank][:, 0:256], AF.Exp),
                deps=[g7])
            pm.read(bank, e7)
            s3_toks.append(e7)
        for mt in range(2):
            bank = 4 + mt
            g8 = acc_group(b, pm, bank,
                           [(es3[:, jt * 256 + mt * 128: jt * 256 + (mt + 1) * 128],
                             v0p[:, jt * 65: jt * 65 + 65]) for jt in range(32)],
                           s3_toks + v_toks, out=pm.t[bank][:, 0:65])
            e8 = b.op("vector", lambda q, bank=bank, mt=mt: q.reciprocal(
                t1f[:, mt * 65 + 64: mt * 65 + 65], pm.t[bank][:, 64:65]), deps=[g8])
            e8b = b.op("scalar", lambda q, bank=bank, mt=mt: q.activation(
                t1f[:, mt * 65: mt * 65 + 64], pm.t[bank][:, 0:64], AF.Copy,
                scale=t1f[:, mt * 65 + 64: mt * 65 + 65]), deps=[e8])
            pm.read(bank, e8b)
        t_t1b = b.op("vector", lambda q, h=h: q.tensor_copy(
            t1b[:, h * 128: h * 128 + 64], t1f[:, 0:64]), deps=[b.now("q_scalar")])
        t_t1b2 = b.op("vector", lambda q, h=h: q.tensor_copy(
            t1b[:, h * 128 + 64: h * 128 + 128], t1f[:, 65:129]), deps=[b.now("q_scalar")])
        t2_toks = []
        for mt in range(2):
            bank = 4 + mt
            g9 = acc_group(b, pm, bank,
                           [(ztres[:, sq_off(h, kt) + mt * 128: sq_off(h, kt) + (mt + 1) * 128],
                             t1b[:, h * 128 + kt * 64: h * 128 + (kt + 1) * 64])
                            for kt in range(2)],
                           [t_t1b, t_t1b2] + zt_final, out=pm.t[bank][:, 0:64])
            e9 = b.op("vector", lambda q, h=h, mt=mt, bank=bank: q.tensor_copy(
                t2p[:, (h * 2 + mt) * 65: (h * 2 + mt) * 65 + 64], pm.t[bank][:, 0:64]),
                deps=[g9])
            pm.read(bank, e9)
            t2_toks.append(e9)
        t_t2o = b.op("vector", lambda q, h=h: q.memset(
            v3(t2p, (128, 2, 1), (2 * 8 * 65, 65, 1), off=(h * 2) * 65 + 64), 1.0),
            deps=t2_toks)
        p1_toks = []
        for mt in range(2):
            for ch in range(8):
                bank = 2 + (mt * 8 + ch) % 2
                g10 = acc_group(b, pm, bank,
                                [(kl0[0:64, mt * 128:(mt + 1) * 128],
                                  q0b[:, ch * 512:(ch + 1) * 512])],
                                qk_toks if (mt == 0 and ch == 0) else [],
                                out=pm.t[bank][:, 0:512])
                e10 = b.op("scalar", lambda q, mt=mt, ch=ch, bank=bank: q.activation(
                    p1t[:, mt * T + ch * 512: mt * T + (ch + 1) * 512],
                    pm.t[bank][:, 0:512], AF.Exp), deps=[g10])
                pm.read(bank, e10)
                p1_toks.append(e10)
        for it_ in range(32):
            bank = 4 + it_ % 2
            g11 = acc_group(b, pm, bank,
                            [(p1t[:, mt * T + it_ * 128:mt * T + (it_ + 1) * 128],
                              t2p[:, (h * 2 + mt) * 65: (h * 2 + mt + 1) * 65])
                             for mt in range(2)],
                            p1_toks + t2_toks + [t_t2o] if it_ == 0 else [],
                            out=pm.t[bank][:, 0:65])
            e11 = b.op("vector", lambda q, bank=bank: q.reciprocal(
                t1f[:, 130:131], pm.t[bank][:, 64:65]), deps=[g11])
            e11b = b.op("scalar", lambda q, bank=bank: q.activation(
                t1f[:, 131:195], pm.t[bank][:, 0:64], AF.Copy,
                scale=t1f[:, 130:131]), deps=[e11])
            pm.read(bank, e11b)
            e11c = b.op("vector", lambda q, h=h, it_=it_: q.tensor_tensor(
                outall[:, it_ * 512 + h * 64: it_ * 512 + (h + 1) * 64],
                t1f[:, 131:195], convt[:, it_ * 64:(it_ + 1) * 64], ALU.add),
                deps=[e11b, cv_toks[it_]])
            head_toks.append(e11c)
    s1_keep = [b.now("q_tensor"), b.now("q_vector"), b.now("q_scalar")]
    s1.close()

    s2 = contextlib.ExitStack()
    outallT = b.sbuf("outallT", [128, 4 * T], BF16, stack=s2)
    outw_sb = b.sbuf("outw_sb", [128, 4 * DM], BF16, stack=s2)
    outb_sb = b.sbuf("outb_sb", [128, NKT], F32, stack=s2)
    xg0f = b.sbuf("xg0f", [128, NKT * 512], F32, stack=s2)
    t_ow = dma(b, "gpsimd", outw_sb[:].rearrange("p (kt f) -> p kt f", f=DM),
               outw.rearrange("(kt p) f -> p kt f", p=128), deps=s1_keep, sem="lwo")
    t_ob = dma(b, "sync", outb_sb[:], outb, deps=s1_keep, sem="lwo2")
    trT = []
    for tt in range(32):
        for kt in range(4):
            bank = (tt * 4 + kt) % 2
            tokm = acc_group(b, pm, bank,
                             [(outall[:, tt * 512 + kt * 128: tt * 512 + (kt + 1) * 128],
                               identb[:])], head_toks + [t_id] if (tt == 0 and kt == 0) else [],
                             out=pm.t[bank][:, 0:128])
            ev = b.op("vector", lambda q, tt=tt, kt=kt, bank=bank: q.tensor_copy(
                outallT[:, kt * T + tt * 128: kt * T + (tt + 1) * 128],
                pm.t[bank][:, 0:128]), deps=[tokm])
            pm.read(bank, ev)
            trT.append(ev)
    xg_store = []
    for ch in range(8):
        xg_toks = []
        for ft in range(NKT):
            bank = 2 + ft % 2
            tokm = acc_group(b, pm, bank,
                             [(outw_sb[:, k * DM + ft * 128: k * DM + (ft + 1) * 128],
                               outallT[:, k * T + ch * 512: k * T + (ch + 1) * 512])
                              for k in range(4)],
                             [t_ow] + (trT if (ch == 0 and ft == 0) else []))
            ev = b.op("vector", lambda q, ft=ft, bank=bank: q.tensor_scalar(
                xg0f[:, ft * 512:(ft + 1) * 512], pm.t[bank][:],
                outb_sb[:, ft:ft + 1], None, ALU.add),
                deps=[tokm, t_ob] + (xg_store[-1:] if xg_store else []))
            pm.read(bank, ev)
            xg_toks.append(ev)
        xg_store.append(dma(b, "sync", xg0T_out[:, ch * 512:(ch + 1) * 512].rearrange(
            "(kt p) t -> p kt t", p=128),
            xg0f[:].rearrange("p (kt t) -> p kt t", t=512), deps=xg_toks, sem="sxg"))
    ph2_done = [b.now("q_tensor"), b.now("q_vector")] + xg_store[-1:]
    s2.close()

    s3 = contextlib.ExitStack()
    xch = b.sbuf("xch", [128, NKT * 512], F32, stack=s3)
    encb = b.sbuf("encb", [128, NKT * 512], BF16, stack=s3)
    wq_sb = b.sbuf("wq_sb", [128, NKT * 256], BF16, stack=s3)
    wk_sb = b.sbuf("wk_sb", [128, NKT * 256], BF16, stack=s3)
    qet = b.sbuf("qet", [128, 2 * 512], BF16, stack=s3)
    ket = b.sbuf("ket", [128, 2 * 512], BF16, stack=s3)
    qtok = b.sbuf("qtok", [128, 4 * 256], BF16, stack=s3)
    ktok = b.sbuf("ktok", [128, 4 * 256], BF16, stack=s3)
    t_wq2 = dma(b, "gpsimd", wq_sb[:].rearrange("p (kt f) -> p kt f", f=256),
                wqin.rearrange("(kt p) f -> p kt f", p=128), deps=ph2_done, sem="lw2a")
    t_wk2 = dma(b, "gpsimd", wk_sb[:].rearrange("p (kt f) -> p kt f", f=256),
                wkin.rearrange("(kt p) f -> p kt f", p=128), deps=ph2_done, sem="lw2b")
    qk_store = []
    for ch in range(8):
        cdep = ph2_done + [b.now("q_tensor"), b.now("q_vector")] + qk_store[-1:]
        t_x0c = dma(b, "sync", xch[:].rearrange("p (kt t) -> p kt t", t=512),
                    x20T[:, ch * 512:(ch + 1) * 512].rearrange("(kt p) t -> p kt t", p=128),
                    deps=cdep, sem="lx0c")
        t_xg0c = dma(b, "gpsimd", encb[:].rearrange("p (kt t) -> p kt t", t=512),
                     xg0T_out[:, ch * 512:(ch + 1) * 512].rearrange(
                         "(kt p) t -> p kt t", p=128),
                     deps=cdep + xg_store[-1:], sem="lxg")
        enc_toks = [b.op("vector", lambda q, ft=ft: q.tensor_tensor(
            encb[:, ft * 512:(ft + 1) * 512], encb[:, ft * 512:(ft + 1) * 512],
            xch[:, ft * 512:(ft + 1) * 512], ALU.add), deps=[t_x0c, t_xg0c])
            for ft in range(NKT)]
        for mt in range(2):
            bank = mt % 2
            g1 = acc_group(b, pm, bank,
                           [(wq_sb[:, k * 256 + mt * 128: k * 256 + (mt + 1) * 128],
                             encb[:, k * 512:(k + 1) * 512]) for k in range(NKT)],
                           [t_wq2] + enc_toks)
            e1 = b.op("vector", lambda q, mt=mt, bank=bank: q.tensor_copy(
                qet[:, mt * 512:(mt + 1) * 512], pm.t[bank][:]), deps=[g1])
            pm.read(bank, e1)
            g2 = acc_group(b, pm, 2 + mt,
                           [(wk_sb[:, k * 256 + mt * 128: k * 256 + (mt + 1) * 128],
                             encb[:, k * 512:(k + 1) * 512]) for k in range(NKT)],
                           [t_wk2] + enc_toks)
            e2 = b.op("vector", lambda q, mt=mt, bank=bank: q.tensor_copy(
                ket[:, mt * 512:(mt + 1) * 512], pm.t[2 + mt][:]), deps=[g2])
            pm.read(2 + mt, e2)
        tr2 = []
        for tt in range(4):
            for mt in range(2):
                bank = 4 + (tt * 2 + mt) % 2
                tokm = acc_group(b, pm, bank,
                                 [(qet[:, mt * 512 + tt * 128: mt * 512 + (tt + 1) * 128],
                                   identb[:])], [b.now("q_vector")],
                                 out=pm.t[bank][:, 0:128])
                ev = b.op("vector", lambda q, tt=tt, mt=mt, bank=bank: q.tensor_copy(
                    qtok[:, tt * 256 + mt * 128: tt * 256 + (mt + 1) * 128],
                    pm.t[bank][:, 0:128]), deps=[tokm])
                pm.read(bank, ev)
                tokm2 = acc_group(b, pm, 6 + (tt * 2 + mt) % 2,
                                  [(ket[:, mt * 512 + tt * 128: mt * 512 + (tt + 1) * 128],
                                    identb[:])], [b.now("q_vector")],
                                  out=pm.t[6 + (tt * 2 + mt) % 2][:, 0:128])
                ev2 = b.op("vector", lambda q, tt=tt, mt=mt, bank=bank: q.tensor_copy(
                    ktok[:, tt * 256 + mt * 128: tt * 256 + (mt + 1) * 128],
                    pm.t[6 + (tt * 2 + mt) % 2][:, 0:128]), deps=[tokm2])
                pm.read(6 + (tt * 2 + mt) % 2, ev2)
                tr2 += [ev, ev2]
        qk_store.append(dma(b, "sync", qed[ch * 512:(ch + 1) * 512, :].rearrange(
            "(tt p) d -> p tt d", p=128),
            qtok[:].rearrange("p (tt d) -> p tt d", d=256), deps=tr2, sem="sqk"))
        qk_store.append(dma(b, "sync", ked[ch * 512:(ch + 1) * 512, :].rearrange(
            "(tt p) d -> p tt d", p=128),
            ktok[:].rearrange("p (tt d) -> p tt d", d=256), deps=tr2, sem="sqk"))
    ph3_done = [b.now("q_tensor"), b.now("q_vector")] + qk_store[-2:]
    s3.close()

    s4 = contextlib.ExitStack()
    er_i = b.sbuf("er_i", [128, 128], I32, stack=s4)
    ec_i = b.sbuf("ec_i", [128, 128], I32, stack=s4)
    ev_f = b.sbuf("ev_f", [128, 128], F32, stack=s4)
    er_f = b.sbuf("er_f", [128, 128], F32, stack=s4)
    er_d = b.sbuf("er_d", [128, 128], I32, stack=s4)
    rdv_f = b.sbuf("rdv_f", [128, 128], F32, stack=s4)
    rmd_f = b.sbuf("rmd_f", [128, 128], F32, stack=s4)
    cc_u = b.sbuf("cc_u", [128, 128], U32, stack=s4)
    rr_u = b.sbuf("rr_u", [128, 128], U32, stack=s4)
    i128r = b.sbuf("i128r", [128, 128], F32, stack=s4)
    i32r = b.sbuf("i32r", [128, 32], F32, stack=s4)
    qr = b.sbuf("qr", [128, 256], BF16, stack=s4)
    kc = b.sbuf("kc", [128, 256], BF16, stack=s4)
    pr = b.sbuf("pr", [128, 256], F32, stack=s4)
    cont = b.sbuf("cont", [128, 4], F32, stack=s4)
    ohm = b.sbuf("ohm", [128, 128], F32, stack=s4)
    ohd = b.sbuf("ohd", [128, 32], F32, stack=s4)
    Asb = b.sbuf("Asb", [128, 32], F32, stack=s4)
    wv_sb = b.sbuf("wv_sb", [128, NKT * DM], BF16, stack=s4)
    wvb_sb = b.sbuf("wvb_sb", [128, NKT], F32, stack=s4)
    xch2 = b.sbuf("xch2", [128, NKT * 512], F32, stack=s4)
    xbb = b.sbuf("xbb", [128, NKT * 512], BF16, stack=s4)
    vout = b.sbuf("vout", [128, NKT * 512], F32, stack=s4)

    t_er = dma(b, "sync", er_i[:], erows, deps=ph3_done, sem="lde")
    t_ec = dma(b, "sync", ec_i[:], ecols, deps=ph3_done, sem="lde")
    t_ev = dma(b, "sync", ev_f[:], evals, deps=ph3_done, sem="lde")
    t_er = t_ec = t_ev
    b.regs2 = {}

    def _init_regs2(q):
        r = q.alloc_register()
        b.regs2["tok"] = r
        return q.reg_mov(r, T - 1)
    t_rg2 = b.op("gpsimd", _init_regs2, deps=ph3_done)
    t_i1 = b.op("gpsimd", lambda q: q.iota(i128r[:], pattern=[[1, 128]], base=0,
               channel_multiplier=0, allow_small_or_imprecise_dtypes=True),
               deps=ph3_done)
    t_i2 = b.op("gpsimd", lambda q: q.iota(i32r[:], pattern=[[1, 32]], base=0,
               channel_multiplier=0, allow_small_or_imprecise_dtypes=True),
               deps=ph3_done)
    t_c1 = b.op("vector", lambda q: q.tensor_copy(er_f[:], er_i[:]), deps=[t_er])
    t_c2 = b.op("vector", lambda q: q.tensor_scalar(
        er_d[:], er_i[:], 7, None, ALU.logical_shift_right), deps=[t_er])
    t_c2b = b.op("vector", lambda q: q.tensor_copy(rdv_f[:], er_d[:]), deps=[t_c2])
    t_c3 = b.op("vector", lambda q: q.scalar_tensor_tensor(
        rmd_f[:], rdv_f[:], -128.0, er_f[:], ALU.mult, ALU.add), deps=[t_c2b, t_c1])
    t_c4 = b.op("vector", lambda q: q.tensor_copy(cc_u[:], ec_i[:]), deps=[t_ec])
    t_c4b = b.op("vector", lambda q: q.tensor_scalar(
        cc_u[:], cc_u[:], 0, None, ALU.bypass), deps=[t_c4]) if False else None
    t_c5 = b.op("vector", lambda q: q.tensor_copy(pr[:, 0:128], ec_i[:]), deps=[t_ec])
    t_c6 = b.op("vector", lambda q: q.tensor_scalar(
        pr[:, 0:128], pr[:, 0:128], float(T - 1), None, ALU.min), deps=[t_c5])
    t_c7 = b.op("vector", lambda q: q.tensor_copy(cc_u[:], pr[:, 0:128]), deps=[t_c6])
    t_c8 = b.op("vector", lambda q: q.tensor_scalar(
        pr[:, 0:128], er_f[:], float(T - 1), None, ALU.min), deps=[t_c1, t_c7])
    t_c9 = b.op("vector", lambda q: q.tensor_copy(rr_u[:], pr[:, 0:128]), deps=[t_c8])
    edge_pre = [t_c3, t_c7, t_c9, t_ev, t_rg2, t_i1, t_i2]

    A_done = None
    first_mm = True
    prev_g = []
    for te in range(128):
        g1 = b.op("gpsimd", lambda q, te=te: q.indirect_dma_start(
            out=qr[:], out_offset=None, in_=qed,
            in_offset=bass.IndirectOffsetOnAxis(ap=rr_u[:, te:te + 1], axis=0),
            bounds_check=b.regs2["tok"], oob_is_err=False),
            deps=edge_pre + prev_g, sem="ge", dma=True)
        g2 = b.op("gpsimd", lambda q, te=te: q.indirect_dma_start(
            out=kc[:], out_offset=None, in_=ked,
            in_offset=bass.IndirectOffsetOnAxis(ap=cc_u[:, te:te + 1], axis=0),
            bounds_check=b.regs2["tok"], oob_is_err=False),
            deps=[], sem="ge", dma=True)
        t_p = b.op("vector", lambda q: q.tensor_tensor(pr[:, 0:256], qr[:], kc[:],
                                                       ALU.mult), deps=[g1, g2])
        t_r = b.op("vector", lambda q: q.tensor_reduce(cont[:, 0:1], pr[:, 0:256],
                                                       axis=AX.X, op=ALU.add), deps=[t_p])
        t_v = b.op("vector", lambda q, te=te: q.tensor_tensor(
            cont[:, 1:2], cont[:, 0:1], ev_f[:, te:te + 1], ALU.mult), deps=[t_r])
        t_sc = b.op("vector", lambda q: q.tensor_scalar(
            cont[:, 2:3], cont[:, 1:2], 1.0 / 16.0, None, ALU.mult), deps=[t_v])
        t_o1 = b.op("vector", lambda q, te=te: q.tensor_scalar(
            ohm[:], i128r[:], rmd_f[:, te:te + 1], None, ALU.is_equal),
            deps=[t_sc] + prev_g)
        t_o2 = b.op("vector", lambda q: q.tensor_scalar(
            ohm[:], ohm[:], cont[:, 2:3], None, ALU.mult), deps=[t_o1])
        t_o3 = b.op("vector", lambda q, te=te: q.tensor_scalar(
            ohd[:], i32r[:], rdv_f[:, te:te + 1], None, ALU.is_equal), deps=[t_o2])
        g3 = mm(b, pm.t[7][:, 0:32], ohm[:], ohd[:], start=first_mm, stop=(te == 127),
                deps=[t_o3])
        first_mm = False
        prev_g = [g3]
    t_A = b.op("vector", lambda q: q.tensor_copy(Asb[:], pm.t[7][:, 0:32]),
               deps=prev_g)
    pm.read(7, t_A)
    s_A = dma(b, "sync", A_out, Asb[:], deps=[t_A], sem="sA")

    t_wv2 = dma(b, "gpsimd", wv_sb[:].rearrange("p (kt f) -> p kt f", f=DM),
                wvw.rearrange("(kt p) f -> p kt f", p=128), deps=ph3_done, sem="lwv")
    t_vb = dma(b, "sync", wvb_sb[:], wvb, deps=ph3_done, sem="lwv2")
    v_store = []
    for ch in range(8):
        cdep = ph3_done + [b.now("q_tensor"), b.now("q_vector")] + v_store[-1:]
        t_xc = dma(b, "sync", xch2[:].rearrange("p (kt t) -> p kt t", t=512),
                   x2T[:, ch * 512:(ch + 1) * 512].rearrange("(kt p) t -> p kt t", p=128),
                   deps=cdep, sem="lxc")
        t_xb = b.op("vector", lambda q: q.tensor_copy(xbb[:], xch2[:]), deps=[t_xc])
        vt = []
        for ft in range(NKT):
            bank = ft % 4
            tokm = acc_group(b, pm, bank,
                             [(wv_sb[:, k * DM + ft * 128: k * DM + (ft + 1) * 128],
                               xbb[:, k * 512:(k + 1) * 512]) for k in range(NKT)],
                             [t_wv2, t_xb])
            ev = b.op("vector", lambda q, ft=ft, bank=bank: q.tensor_scalar(
                vout[:, ft * 512:(ft + 1) * 512], pm.t[bank][:],
                wvb_sb[:, ft:ft + 1], None, ALU.add), deps=[tokm, t_vb])
            pm.read(bank, ev)
            vt.append(ev)
        v_store.append(dma(b, "sync", valT_out[:, ch * 512:(ch + 1) * 512].rearrange(
            "(kt p) t -> p kt t", p=128),
            vout[:].rearrange("p (kt t) -> p kt t", t=512), deps=vt, sem="svl"))
    s4.close()
    b.op("sync", lambda q: q.nop(), deps=b.barrier_tokens())
    b.build_block()
    return nc, es


def prep_D(inputs, resC):
    identb, identf, utrib = _consts()
    x20T = resC[0]["x2T"]
    attn20 = resC[0]["attn2m"]
    mxall = np.zeros((1, 16), np.float32)
    for c in range(NCORES):
        mxall[0, c] = resC[c]["mx"][0, 0]
        mxall[0, 8 + c] = resC[c]["mx"][0, 1]
    cw = inputs["L1_attn_conv_w"]
    bands = np.zeros((8 * 160, 128), np.float32)
    for h in range(8):
        for i in range(128):
            for k in range(33):
                bands[h * 160 + i + k, i] = cw[h, 0, k, 0]
    rows = inputs["L1_attn_adj_rows"]; cols = inputs["L1_attn_adj_cols"]
    vals = inputs["L1_attn_adj_vals"]
    NE = 131072 // NCORES
    com = {
        "x20T": x20T, "attn20": attn20, "mxall": mxall,
        "qkvw": inputs["L1_attn_qkv_w"], "outw": inputs["L1_attn_out_w"],
        "outb": np.ascontiguousarray(inputs["L1_attn_out_b"].reshape(NKT, 128).T),
        "wqin": inputs["L1_attn_wq"], "wkin": inputs["L1_attn_wk"],
        "wvw": inputs["L1_attn_wv_w"],
        "wvb": np.ascontiguousarray(inputs["L1_attn_wv_b"].reshape(NKT, 128).T),
        "bands": bands, "identb": identb, "identf": identf,
    }
    in_maps = []
    for c in range(NCORES):
        sl = slice(c * NE, (c + 1) * NE)
        in_maps.append(dict(
            com, x2T=resC[c]["x2T"],
            erows=np.ascontiguousarray(rows[sl].reshape(128, 128).T),
            ecols=np.ascontiguousarray(cols[sl].reshape(128, 128).T),
            evals=np.ascontiguousarray(vals[sl].reshape(128, 128).T),
        ))
    return in_maps




def build_launch_E():
    nc = bass.Bass("TRN2", target_bir_lowering=False)
    es = contextlib.ExitStack()
    b = B(nc, es)
    C = _C()

    x2T = nc.dram_tensor("x2T", [DM, T], F32, kind="ExternalInput").ap()
    xg0T = nc.dram_tensor("xg0T", [DM, T], F32, kind="ExternalInput").ap()
    valT = nc.dram_tensor("valT", [DM, T], F32, kind="ExternalInput").ap()
    A_all = nc.dram_tensor("A_all", [128, 8 * 32], F32, kind="ExternalInput").ap()
    x3T_out = nc.dram_tensor("x3T", [DM, T], F32, kind="ExternalOutput").ap()
    arow_d = nc.dram_tensor("arow_d", [T, 1], F32, kind="Internal").ap()

    pm = PsumMgr(b)
    aall = b.sbuf("aall", [128, 8 * 32], F32)
    agr = b.sbuf("agr", [128, 32], F32)
    arow = b.sbuf("arow", [1, T], F32)
    rows = b.sbuf("rows", [1, 8], F32)
    onesrf = b.sbuf("onesrf", [1, 128], F32)
    onescf = b.sbuf("onescf", [128, 1], F32)
    C.onesc = b.sbuf("onesc", [128, 1], BF16)
    C.onesr = b.sbuf("onesr", [1, 128], BF16)
    C.rowf = b.sbuf("rowf", [1, 4 * 512], F32)
    C.rowsb = b.sbuf("rowsb", [1, 2 * 512], BF16)
    C.stb = b.sbuf("stb", [128, NKT * 512], BF16)
    C.sqb = b.sbuf("sqb", [128, NKT * 512], BF16)
    xc = b.sbuf("xc", [128, NKT * 512], F32)
    gc = b.sbuf("gc", [128, NKT * 512], F32)
    vc = b.sbuf("vc", [128, NKT * 512], F32)
    sc = b.sbuf("sc", [128, NKT * 512], F32)
    swb = b.sbuf("swb", [128, NKT * 512], F32)
    x3b = b.sbuf("x3b", [128, NKT * 512], F32)

    t_a = dma(b, "sync", aall[:], A_all, sem="ldc")
    C.t_ones = b.op("vector", lambda q: q.memset(C.onesc[:], 1.0))
    C.t_onesr = b.op("vector", lambda q: q.memset(C.onesr[:], 1.0))
    t_orf = b.op("vector", lambda q: q.memset(onesrf[:], 1.0))
    t_ocf = b.op("vector", lambda q: q.memset(onescf[:], 1.0))
    t_s = b.op("vector", lambda q: q.tensor_copy(agr[:], aall[:, 0:32]), deps=[t_a])
    for c in range(1, 8):
        t_s = b.op("vector", lambda q, c=c: q.tensor_tensor(
            agr[:], agr[:], aall[:, c * 32:(c + 1) * 32], ALU.add), deps=[t_s])
    t_e = b.op("scalar", lambda q: q.activation(agr[:], agr[:], AF.Exp), deps=[t_s])
    t_r1 = b.op("vector", lambda q: q.tensor_reduce(
        C.rowf[0:1, 0:1] if False else swb[:, 0:1], agr[:], axis=AX.X, op=ALU.add),
        deps=[t_e])
    tok = acc_group(b, pm, 6, [(swb[:, 0:1], onescf[:] if False else swb[:, 0:1])],
                    [t_r1], out=pm.t[6][0:1, 0:1]) if False else None
    tok1 = acc_group(b, pm, 6, [(onescf[:], swb[:, 0:1])], [t_r1, t_ocf],
                     out=pm.t[6][0:1, 0:1])
    t_tot = b.op("vector", lambda q: q.reciprocal(rows[0:1, 0:1], pm.t[6][0:1, 0:1]),
                 deps=[tok1])
    pm.read(6, t_tot)
    tok2 = acc_group(b, pm, 6, [(onesrf[:], rows[0:1, 0:1])], [t_tot, t_orf],
                     out=pm.t[6][:, 0:1])
    t_rc = b.op("vector", lambda q: q.tensor_copy(swb[:, 1:2], pm.t[6][:, 0:1]),
                deps=[tok2])
    pm.read(6, t_rc)
    t_al = b.op("scalar", lambda q: q.activation(agr[:], agr[:], AF.Copy,
                                                 scale=swb[:, 1:2]), deps=[t_rc])
    identf = b.sbuf("identf_e", [128, 128], F32)
    t_idf = b.op("gpsimd", lambda q: q.iota(identf[:], pattern=[[1, 128]], base=0,
                 channel_multiplier=0, allow_small_or_imprecise_dtypes=True)) \
        if False else None
    t_one = b.op("vector", lambda q: q.memset(identf[:], 1.0))
    t_aff = b.op("gpsimd", lambda q: q.affine_select(
        identf[:], identf[:], pattern=[[-1, 128]], compare_op=ALU.is_equal,
        fill=0.0, base=0, channel_multiplier=1), deps=[t_one])
    tok3 = acc_group(b, pm, 6, [(agr[:, 0:32], identf[:])], [t_al, t_aff],
                     out=pm.t[6][0:32, 0:128])
    t_agt = b.op("vector", lambda q: q.tensor_copy(swb[0:32, 2:130], pm.t[6][0:32, 0:128]),
                 deps=[tok3])
    pm.read(6, t_agt)
    s_ar = dma(b, "sync", arow_d.rearrange("(c p) one -> c (p one)", p=128),
               swb[0:32, 2:130], deps=[t_agt], sem="sar")
    l_ar = dma(b, "sync", arow[:], bass.AP(arow_d.tensor, arow_d.offset,
                                           [[T, 1], [1, T]]),
               deps=[s_ar], sem="lar")

    prev = []
    for ch in range(8):
        cdep = [b.now("q_tensor"), b.now("q_vector"), b.now("q_scalar")] + prev
        t_x = dma(b, "sync", xc[:].rearrange("p (kt t) -> p kt t", t=512),
                  x2T[:, ch * 512:(ch + 1) * 512].rearrange("(kt p) t -> p kt t", p=128),
                  deps=cdep, sem="ldch")
        t_g = dma(b, "sync", gc[:].rearrange("p (kt t) -> p kt t", t=512),
                  xg0T[:, ch * 512:(ch + 1) * 512].rearrange("(kt p) t -> p kt t", p=128),
                  deps=cdep, sem="ldch")
        t_v = dma(b, "sync", vc[:].rearrange("p (kt t) -> p kt t", t=512),
                  valT[:, ch * 512:(ch + 1) * 512].rearrange("(kt p) t -> p kt t", p=128),
                  deps=cdep, sem="ldch")
        tokb = acc_group(b, pm, 6, [(onesrf[:], arow[0:1, ch * 512:(ch + 1) * 512])],
                         [l_ar, t_orf] + cdep, out=pm.t[6][:, 0:512])
        t_ab = b.op("vector", lambda q: q.tensor_copy(swb[:, 0:512], pm.t[6][:, 0:512]),
                    deps=[tokb])
        pm.read(6, t_ab)
        ch_toks = []
        for ft in range(NKT):
            sl_ = slice(ft * 512, (ft + 1) * 512)
            a1 = b.op("vector", lambda q, sl_=sl_: q.tensor_tensor(
                sc[:, sl_], vc[:, sl_], swb[:, 0:512], ALU.mult), deps=[t_v, t_ab])
            a2 = b.op("vector", lambda q, sl_=sl_: q.tensor_tensor(
                gc[:, sl_], gc[:, sl_], xc[:, sl_], ALU.add), deps=[t_g, t_x])
            a3 = b.op("scalar", lambda q, sl_=sl_: q.activation(
                swb[:, 512:1024] if False else C.sqb[:, sl_], sc[:, sl_], AF.Sigmoid,
                scale=-1.0), deps=[a1])
            ch_toks.append((sl_, a1, a2, a3))
        out_toks = []
        for ft in range(NKT):
            sl_ = slice(ft * 512, (ft + 1) * 512)
            _, a1, a2, _ = ch_toks[ft]
            w1 = b.op("scalar", lambda q, sl_=sl_: q.activation(
                swb[:, 512:1024], sc[:, sl_], AF.Sigmoid, scale=-1.0),
                deps=[a1] + (out_toks[-1:] if out_toks else []))
            w2 = b.op("vector", lambda q: q.tensor_tensor(
                swb[:, 512:1024], swb[:, 512:1024], swb[:, 512:1024], ALU.mult),
                deps=[w1])
            d1 = b.op("vector", lambda q, sl_=sl_: q.tensor_tensor(
                x3b[:, sl_], sc[:, sl_], gc[:, sl_], ALU.subtract), deps=[w2, a2])
            d2 = b.op("vector", lambda q, sl_=sl_: q.tensor_tensor(
                x3b[:, sl_], x3b[:, sl_], swb[:, 512:1024], ALU.mult), deps=[d1])
            d3 = b.op("vector", lambda q, sl_=sl_: q.tensor_tensor(
                x3b[:, sl_], x3b[:, sl_], gc[:, sl_], ALU.add), deps=[d2])
            d4 = b.op("vector", lambda q, sl_=sl_: q.scalar_tensor_tensor(
                sc[:, sl_], x3b[:, sl_], 2.0, xc[:, sl_], ALU.mult, ALU.add),
                deps=[d3])
            out_toks.append(d4)
        bm, br = ln_stats(b, pm, C, sc[:], out_toks, 7, 4, 5)
        x3_toks = []
        for ft in range(NKT):
            sl_ = slice(ft * 512, (ft + 1) * 512)
            a1 = b.op("vector", lambda q, sl_=sl_: q.tensor_tensor(
                x3b[:, sl_], sc[:, sl_], pm.t[4][:], ALU.subtract),
                deps=[bm] + out_toks)
            a2 = b.op("vector", lambda q, sl_=sl_: q.tensor_tensor(
                x3b[:, sl_], x3b[:, sl_], pm.t[5][:], ALU.mult), deps=[br, a1])
            pm.read(4, a1)
            pm.read(5, a2)
            x3_toks.append(a2)
        prev = [dma(b, "sync", x3T_out[:, ch * 512:(ch + 1) * 512].rearrange(
            "(kt p) t -> p kt t", p=128),
            x3b[:].rearrange("p (kt t) -> p kt t", t=512), deps=x3_toks, sem="sx3")]
    b.op("sync", lambda q: q.nop(), deps=b.barrier_tokens())
    b.build_block()
    return nc, es


def prep_E(resC, resD):
    A_all = np.concatenate([resD[c]["A_c"] for c in range(NCORES)], axis=1)
    xg0T = resD[0]["xg0T"]
    return [{"x2T": resC[c]["x2T"], "xg0T": xg0T, "valT": resD[c]["valT"],
             "A_all": A_all} for c in range(NCORES)]




DI = 2730
NTL = 22


def _pair_cols(tl):
    return (tl * 256, 128 if tl < 21 else 42)


def build_launch_F(nlayers, final_ln):
    nc = bass.Bass("TRN2", target_bir_lowering=False)
    es = contextlib.ExitStack()
    b = B(nc, es)
    C = _C()

    xin = nc.dram_tensor("xin", [DM, T], F32, kind="ExternalInput").ap()
    x3in = nc.dram_tensor("x3Tin", [DM, T], F32, kind="ExternalInput").ap()
    wqkv = [nc.dram_tensor(f"wqkv{i}", [DM, 3072], F32, kind="ExternalInput").ap()
            for i in range(nlayers)]
    wo = [nc.dram_tensor(f"wo{i}", [DM, DM], F32, kind="ExternalInput").ap()
          for i in range(nlayers)]
    w1p = [nc.dram_tensor(f"w1p{i}", [DM, 2 * DI], F32, kind="ExternalInput").ap()
           for i in range(nlayers)]
    w2in = [nc.dram_tensor(f"w2{i}", [DI, DM], F32, kind="ExternalInput").ap()
            for i in range(nlayers)]
    b1c = [nc.dram_tensor(f"b1c{i}", [128, 2 * NTL], F32, kind="ExternalInput").ap()
           for i in range(nlayers)]
    b2c = [nc.dram_tensor(f"b2c{i}", [128, NKT], F32, kind="ExternalInput").ap()
           for i in range(nlayers)]
    gcols = [nc.dram_tensor(f"gci{i}", [128, NKT], F32, kind="ExternalInput").ap()
             for i in range(nlayers)]
    bcols = [nc.dram_tensor(f"bci{i}", [128, NKT], F32, kind="ExternalInput").ap()
             for i in range(nlayers)]
    qsin = [nc.dram_tensor(f"qsi{i}", [128, 64], F32, kind="ExternalInput").ap()
            for i in range(nlayers)]
    ksin = [nc.dram_tensor(f"ksi{i}", [128, 64], F32, kind="ExternalInput").ap()
            for i in range(nlayers)]
    tabs = nc.dram_tensor("tabs", [128, 8 * 1024], BF16, kind="ExternalInput").ap()
    masks = nc.dram_tensor("masks", [128, 4 * 256], F32, kind="ExternalInput").ap()
    identb_in = nc.dram_tensor("identb", [128, 128], BF16, kind="ExternalInput").ap()

    xout = nc.dram_tensor("xoutT", [DM, T], F32, kind="ExternalOutput").ap()
    xa = nc.dram_tensor("xa_d", [DM, T], F32, kind="Internal").ap()
    xb_d = nc.dram_tensor("xb_d", [DM, T], F32, kind="Internal").ap()

    pm = PsumMgr(b)
    identb = b.sbuf("identb_sb", [128, 128], BF16)
    tabc = b.sbuf("tabc", [128, 4 * 256], BF16)
    masksb = b.sbuf("masksb", [128, 4 * 256], BF16)
    C.onesc = b.sbuf("onesc", [128, 1], BF16)
    C.onesr = b.sbuf("onesr", [1, 128], BF16)
    onesrf = b.sbuf("onesrf", [1, 128], F32)
    C.rowf = b.sbuf("rowf", [1, 4 * 512], F32)
    C.rowsb = b.sbuf("rowsb", [1, 2 * 512], BF16)
    C.stb = b.sbuf("stb", [128, NKT * 512], BF16)
    C.sqb = b.sbuf("sqb", [128, NKT * 512], BF16)
    xc = b.sbuf("xc", [128, NKT * 512], F32)
    xn = b.sbuf("xn", [128, NKT * 512], F32)
    xsb = b.sbuf("xsb", [128, NKT * 512], BF16)
    wst = [b.sbuf(f"wst{i}", [128, NKT * 512], BF16) for i in range(2)]
    qsb = b.sbuf("qsb", [128, 64], F32)
    ksb = b.sbuf("ksb", [128, 64], F32)
    dsc = b.sbuf("dsc", [1, 512], F32)
    avf = b.sbuf("avf", [64, 512], F32)
    qkvb = b.sbuf("qkvb", [128, 4 * 3072], BF16)
    q2b = b.sbuf("q2b", [128, 4 * 2048], BF16)
    rsc2 = b.sbuf("rsc2", [128, 2048], BF16)
    sqf = b.sbuf("sqf", [128, 1024], F32)
    ssf = b.sbuf("ssf", [128, 3 * 64], F32)
    qTb = b.sbuf("qTb", [128, NKT * 512], BF16)
    kTb = b.sbuf("kTb", [128, NKT * 512], BF16)
    kprev = b.sbuf("kprev", [128, NKT * 256], BF16)
    vprev = b.sbuf("vprev", [128, 2 * 1024], BF16)
    ptw = b.sbuf("ptw", [128, 2 * 4 * 256], BF16)
    attoT = b.sbuf("attoT", [128, NKT * 512], BF16)
    wosb = b.sbuf("wosb", [128, NKT * DM], BF16)
    b1sb = b.sbuf("b1sb", [128, 2 * NTL], F32)
    b2sb = b.sbuf("b2sb", [128, NKT], F32)
    gsb = b.sbuf("gsb", [128, NKT], F32)
    bsb = b.sbuf("bsb", [128, NKT], F32)
    prodT = qkvb
    ab1 = qkvb
    AB1 = NTL * 512

    tok0 = dma(b, "sync", identb[:], identb_in, sem="ldc")
    t_cn = dma(b, "gpsimd", masksb[:], masks, sem="ldm")
    C.t_ones = b.op("vector", lambda q: q.memset(C.onesc[:], 1.0))
    C.t_onesr = b.op("vector", lambda q: q.memset(C.onesr[:], 1.0))
    t_orf = b.op("vector", lambda q: q.memset(onesrf[:], 1.0))

    def shifted_load(dst_b, src, ch, deps, dst_f=None):
        toks = []
        if dst_f is not None:
            toks.append(dma(b, "sync", dst_f[:].rearrange("p (kt t) -> p kt t", t=512),
                            src[:, ch * 512:(ch + 1) * 512].rearrange(
                                "(kt p) t -> p kt t", p=128), deps=deps, sem="lxf"))
        toks.append(dma(b, "gpsimd", v3(dst_b, (128, 4, 512), (NKT * 512, 512, 1)),
                        src[0:512, ch * 512:(ch + 1) * 512].rearrange(
                            "(kt p) t -> p kt t", p=128), deps=deps, sem="lxb"))
        if ch == 0:
            toks.append(b.op("vector", lambda q: q.memset(
                v3(dst_b, (128, 4, 1), (NKT * 512, 512, 1), off=4 * 512), 0.0),
                deps=deps))
            toks.append(dma(b, "gpsimd",
                            v3(dst_b, (128, 4, 511), (NKT * 512, 512, 1),
                               off=4 * 512 + 1),
                            src[512:1024, 0:511].rearrange("(kt p) t -> p kt t", p=128),
                            deps=deps, sem="lxb"))
        else:
            toks.append(dma(b, "gpsimd",
                            v3(dst_b, (128, 4, 512), (NKT * 512, 512, 1), off=4 * 512),
                            src[512:1024, ch * 512 - 1:(ch + 1) * 512 - 1].rearrange(
                                "(kt p) t -> p kt t", p=128), deps=deps, sem="lxb"))
        return toks

    cur_in = xin
    out_store = []
    wtok = [None, None]
    wst_rd = [None, None]
    w2_rd = [[], []]
    ptw_rd = [[], []]
    dsc_rd = [[], []]
    for li in range(nlayers):
        cur_mid = xa
        cur_out2 = xb_d if (li < nlayers - 1 or final_ln) else xout
        lay_dep = [b.now("q_tensor"), b.now("q_vector"), b.now("q_scalar"),
                   b.now("q_gpsimd")] + out_store[-1:]
        t_qs = dma(b, "sync", qsb[:], qsin[li], deps=lay_dep, sem=f"lqs{li}")
        t_ks = dma(b, "sync", ksb[:], ksin[li], deps=lay_dep, sem=f"lqs{li}")
        t_qs = t_ks
        t_wo = dma(b, "gpsimd", wosb[:].rearrange("p (kt f) -> p kt f", f=DM),
                   wo[li].rearrange("(kt p) f -> p kt f", p=128),
                   deps=lay_dep, sem=f"lwoF{li}")
        mid_store = []
        for ch in range(8):
            cdep = (lay_dep + [b.now("q_tensor"), b.now("q_vector"), b.now("q_scalar")]
                    + mid_store[-1:])
            ld = shifted_load(xsb, cur_in, ch, cdep, dst_f=xc)
            t_tab = dma(b, "sync", tabc[:], tabs[:, ch * 1024:(ch + 1) * 1024],
                        deps=cdep, sem="ltb")
            qkv_toks = []
            for nchk in range(6):
                buf = nchk % 2
                wtok[buf] = dma(b, "gpsimd",
                                wst[buf][:].rearrange("p (kt f) -> p kt f", f=512),
                                wqkv[li][:, nchk * 512:(nchk + 1) * 512].rearrange(
                                    "(kt p) f -> p kt f", p=128),
                                deps=[wst_rd[buf]], sem=f"lwF{buf}")
                for tt in range(4):
                    bank = (nchk * 4 + tt) % 4
                    tokm = acc_group(b, pm, bank,
                                     [(xsb[:, k * 512 + tt * 128: k * 512 + (tt + 1) * 128],
                                       wst[buf][:, k * 512:(k + 1) * 512])
                                      for k in range(NKT)],
                                     [wtok[buf]] + ld)
                    ev = b.op("vector", lambda q, nchk=nchk, tt=tt, bank=bank:
                        q.tensor_copy(
                            qkvb[:, tt * 3072 + nchk * 512: tt * 3072 + (nchk + 1) * 512],
                            pm.t[bank][:]), deps=[tokm])
                    pm.read(bank, ev)
                    qkv_toks.append(ev)
                wst_rd[buf] = tokm
            nrm_toks = []
            for off, stile in [(0, qsb), (1024, ksb)]:
                for tt in range(4):
                    base = tt * 3072 + off
                    prevn = nrm_toks[-1:] if nrm_toks else []
                    sq = b.op("scalar", lambda q, base=base: q.activation(
                        sqf[:], qkvb[:, base:base + 1024], AF.Square),
                        deps=qkv_toks + prevn)
                    rs = b.op("vector", lambda q: q.tensor_reduce(
                        v3(ssf, (128, 16, 1), (192, 1, 1)),
                        v3(sqf, (128, 16, 64), (1024, 64, 1)),
                        axis=AX.X, op=ALU.add), deps=[sq])
                    sz = b.op("scalar", lambda q: q.activation(
                        ssf[:, 64:80], ssf[:, 0:16], AF.Sqrt), deps=[rs])
                    mx_ = b.op("vector", lambda q: q.tensor_scalar(
                        ssf[:, 64:80], ssf[:, 64:80], 1e-12, None, ALU.max), deps=[sz])
                    rc = b.op("vector", lambda q: q.reciprocal(
                        ssf[:, 128:144], ssf[:, 64:80]), deps=[mx_])
                    m1 = b.op("vector", lambda q, base=base: q.tensor_tensor(
                        v3(qkvb, (128, 16, 64), (4 * 3072, 64, 1), off=base),
                        v3(qkvb, (128, 16, 64), (4 * 3072, 64, 1), off=base),
                        v3(ssf, (128, 16, 64), (192, 1, 0), off=128), ALU.mult),
                        deps=[rc])
                    m2 = b.op("vector", lambda q, base=base, stile=stile: q.tensor_tensor(
                        v3(qkvb, (128, 16, 64), (4 * 3072, 64, 1), off=base),
                        v3(qkvb, (128, 16, 64), (4 * 3072, 64, 1), off=base),
                        v3(stile, (128, 16, 64), (64, 0, 1)), ALU.mult),
                        deps=[m1, t_qs])
                    nrm_toks.append(m2)
            rot_toks = []
            for off, q2off, tbc, tbs in [(0, 0, 0, 256), (1024, 1024, 512, 768)]:
                qa = v3(qkvb, (128, 4, 16, 32), (4 * 3072, 3072, 64, 1), off=off)
                qb2 = v3(qkvb, (128, 4, 16, 32), (4 * 3072, 3072, 64, 1), off=off + 32)
                oa = v3(q2b, (128, 4, 16, 32), (4 * 2048, 2048, 64, 1), off=q2off)
                ob = v3(q2b, (128, 4, 16, 32), (4 * 2048, 2048, 64, 1), off=q2off + 32)
                sc_ = v3(rsc2, (128, 4, 16, 32), (2048, 512, 32, 1))
                cv = v3(tabc, (128, 4, 16, 32), (4 * 256, 64, 0, 1), off=tbc)
                sv = v3(tabc, (128, 4, 16, 32), (4 * 256, 64, 0, 1), off=tbs)
                a1 = b.op("vector", lambda q, qa=qa, cv=cv, oa=oa: q.tensor_tensor(
                    oa, qa, cv, ALU.mult), deps=nrm_toks + rot_toks + [t_tab])
                a2 = b.op("vector", lambda q, qb2=qb2, sv=sv, sc_=sc_: q.tensor_tensor(
                    sc_, qb2, sv, ALU.mult), deps=nrm_toks + rot_toks + [t_tab])
                a3 = b.op("vector", lambda q, oa=oa, sc_=sc_: q.tensor_tensor(
                    oa, oa, sc_, ALU.subtract), deps=[a1, a2])
                a4 = b.op("vector", lambda q, qb2=qb2, cv=cv, ob=ob: q.tensor_tensor(
                    ob, qb2, cv, ALU.mult), deps=[a3])
                a5 = b.op("vector", lambda q, qa=qa, sv=sv, sc_=sc_: q.tensor_tensor(
                    sc_, qa, sv, ALU.mult), deps=[a4])
                a6 = b.op("vector", lambda q, ob=ob, sc_=sc_: q.tensor_tensor(
                    ob, ob, sc_, ALU.add), deps=[a5])
                rot_toks.append(a6)
            tr_toks = []
            for which, q2off, dstb in [(0, 0, qTb), (1, 1024, kTb)]:
                for tt in range(4):
                    for dt in range(NKT):
                        bank = (tt * NKT + dt) % 2
                        tokm = acc_group(b, pm, bank,
                                         [(q2b[:, tt * 2048 + q2off + dt * 128:
                                                tt * 2048 + q2off + (dt + 1) * 128],
                                           identb[:])], [rot_toks[which], tok0],
                                         out=pm.t[bank][:, 0:128])
                        ev = b.op("vector", lambda q, tt=tt, dt=dt, bank=bank,
                                  dstb=dstb: q.tensor_copy(
                            dstb[:, dt * 512 + tt * 128: dt * 512 + (tt + 1) * 128],
                            pm.t[bank][:, 0:128]), deps=[tokm])
                        pm.read(bank, ev)
                        tr_toks.append(ev)
            att_toks = []
            for wi in range(2):
                for h in range(16):
                    hs = h % 2
                    pb = hs * 64
                    po = hs * 1024
                    dt = h // 2
                    glob0 = (ch == 0 and wi == 0)
                    st_toks = []
                    for kt_rel in range(4):
                        if glob0 and kt_rel < 2:
                            st_toks.append(None)
                            continue
                        bank = 2 + kt_rel % 2
                        if wi == 0 and kt_rel < 2:
                            lh = kprev[pb:pb + 64,
                                       dt * 256 + kt_rel * 128: dt * 256 + (kt_rel + 1) * 128]
                        else:
                            koff = (kt_rel - 2) if wi == 0 else kt_rel
                            lh = kTb[pb:pb + 64,
                                     dt * 512 + koff * 128: dt * 512 + (koff + 1) * 128]
                        g1 = acc_group(b, pm, bank,
                                       [(lh, qTb[pb:pb + 64,
                                                 dt * 512 + wi * 256: dt * 512 + wi * 256 + 256])],
                                       tr_toks if (wi == 0 and h == 0) else [],
                                       out=pm.t[bank][:, 0:256])
                        e2 = b.op("scalar", lambda q, kt_rel=kt_rel, bank=bank, po=po:
                            q.activation(
                                ptw[:, po + kt_rel * 256: po + (kt_rel + 1) * 256],
                                pm.t[bank][:, 0:256], AF.Exp, scale=8.0),
                            deps=[g1] + ptw_rd[hs])
                        pm.read(bank, e2)
                        m1 = b.op("gpsimd", lambda q, kt_rel=kt_rel, po=po:
                            q.tensor_tensor(
                                ptw[:, po + kt_rel * 256: po + (kt_rel + 1) * 256],
                                ptw[:, po + kt_rel * 256: po + (kt_rel + 1) * 256],
                                masksb[:, kt_rel * 256:(kt_rel + 1) * 256], ALU.mult),
                            deps=[e2, t_cn])
                        st_toks.append(m1)
                    bank_av = 4 + hs
                    bank_dn = 0 + hs
                    bank_g = 6 + hs
                    first = True
                    gden = gav = None
                    for kt_rel in range(4):
                        if glob0 and kt_rel < 2:
                            continue
                        if wi == 0 and kt_rel < 2:
                            vsl = vprev[:, kt_rel * 1024 + h * 64: kt_rel * 1024 + h * 64 + 64]
                        else:
                            koff = (kt_rel - 2) if wi == 0 else kt_rel
                            vsl = qkvb[:, koff * 3072 + 2048 + h * 64:
                                       koff * 3072 + 2048 + h * 64 + 64]
                        last = (kt_rel == 3)
                        gav = mm(b, pm.t[bank_av][0:64, 0:256], vsl,
                                 ptw[:, po + kt_rel * 256: po + (kt_rel + 1) * 256],
                                 start=first, stop=last,
                                 deps=(pm.war(bank_av) + pm.war(bank_dn)
                                       + [t for t in st_toks if t]) if first else [])
                        gden = mm(b, pm.t[bank_dn][0:1, 0:256], C.onesc[:],
                                  ptw[:, po + kt_rel * 256: po + (kt_rel + 1) * 256],
                                  start=first, stop=last, deps=[C.t_ones] if first else [])
                        first = False
                    pm.wrote(bank_av)
                    pm.wrote(bank_dn)
                    ptw_rd[hs] = [gden]
                    d2 = b.op("vector", lambda q, hs=hs, bank_dn=bank_dn: q.reciprocal(
                        dsc[0:1, hs * 256:(hs + 1) * 256],
                        pm.t[bank_dn][0:1, 0:256]), deps=[gden] + dsc_rd[hs])
                    pm.read(bank_dn, d2)
                    g2 = acc_group(b, pm, bank_g,
                                   [(onesrf[0:1, 0:64], dsc[0:1, hs * 256:(hs + 1) * 256])],
                                   [d2, t_orf], out=pm.t[bank_g][0:64, 0:256])
                    dsc_rd[hs] = [g2]
                    d3 = b.op("vector", lambda q, hs=hs, bank_av=bank_av: q.tensor_copy(
                        avf[:, hs * 256:(hs + 1) * 256],
                        pm.t[bank_av][0:64, 0:256]), deps=[gav])
                    pm.read(bank_av, d3)
                    d4 = b.op("vector", lambda q, pb=pb, dt=dt, wi=wi, hs=hs,
                              bank_g=bank_g: q.tensor_tensor(
                        attoT[pb:pb + 64, dt * 512 + wi * 256: dt * 512 + wi * 256 + 256],
                        avf[0:64, hs * 256:(hs + 1) * 256],
                        pm.t[bank_g][0:64, 0:256], ALU.mult), deps=[d3, g2])
                    pm.read(bank_g, d4)
                    att_toks.append(d4)
            roll_deps = [b.now("q_tensor")]
            rk = b.op("vector", lambda q: q.tensor_copy(
                v3(kprev, (128, NKT, 256), (NKT * 256, 256, 1)),
                v3(kTb, (128, NKT, 256), (NKT * 512, 512, 1), off=256)), deps=roll_deps)
            rv = b.op("vector", lambda q: q.tensor_copy(
                v3(vprev, (128, 2, 1024), (2 * 1024, 1024, 1)),
                v3(qkvb, (128, 2, 1024), (4 * 3072, 3072, 1), off=2 * 3072 + 2048)),
                deps=roll_deps)
            wo_toks = []
            for ft in range(NKT):
                bank = ft % 2
                tokm = acc_group(b, pm, bank,
                                 [(wosb[:, k * DM + ft * 128: k * DM + (ft + 1) * 128],
                                   attoT[:, k * 512:(k + 1) * 512]) for k in range(NKT)],
                                 [t_wo] + att_toks)
                ev = b.op("vector", lambda q, ft=ft, bank=bank: q.tensor_tensor(
                    xn[:, ft * 512:(ft + 1) * 512], pm.t[bank][:],
                    xc[:, ft * 512:(ft + 1) * 512], ALU.add), deps=[tokm] + ld)
                pm.read(bank, ev)
                wo_toks.append(ev)
            mid_store.append(dma(b, "sync",
                                 cur_mid[:, ch * 512:(ch + 1) * 512].rearrange(
                                     "(kt p) t -> p kt t", p=128),
                                 xn[:].rearrange("p (kt t) -> p kt t", t=512),
                                 deps=wo_toks + [rk, rv], sem=f"smid{li}"))
        pass1_done = [b.now("q_tensor"), b.now("q_vector"), b.now("q_scalar"),
                      b.now("q_gpsimd")] + mid_store[-1:]
        w2_rd = [list(pass1_done), list(pass1_done)]

        t_b1 = dma(b, "sync", b1sb[:], b1c[li], deps=pass1_done, sem=f"lb{li}")
        t_b2 = dma(b, "sync", b2sb[:], b2c[li], deps=pass1_done, sem=f"lb{li}")
        t_g = dma(b, "sync", gsb[:], gcols[li], deps=pass1_done, sem=f"lb{li}")
        t_bb = dma(b, "sync", bsb[:], bcols[li], deps=pass1_done, sem=f"lb{li}")
        t_b1 = t_b2 = t_g = t_bb
        out_store = []
        for ch in range(8):
            cdep = (pass1_done + [b.now("q_tensor"), b.now("q_vector"),
                                  b.now("q_scalar")] + out_store[-1:])
            ld = shifted_load(xsb, cur_mid, ch, cdep, dst_f=xc)
            bm, br = ln_stats(b, pm, C, xsb[:], ld, 7, 4, 5)
            ln_toks = []
            for ft in range(NKT):
                sl_ = slice(ft * 512, (ft + 1) * 512)
                a1 = b.op("vector", lambda q, sl_=sl_: q.tensor_tensor(
                    C.stb[:, sl_], xsb[:, sl_], pm.t[4][:], ALU.subtract),
                    deps=[bm] + ld)
                a2 = b.op("vector", lambda q, sl_=sl_: q.tensor_tensor(
                    C.stb[:, sl_], C.stb[:, sl_], pm.t[5][:], ALU.mult),
                    deps=[br, a1])
                a3 = b.op("vector", lambda q, sl_=sl_, ft=ft: q.tensor_scalar(
                    C.stb[:, sl_], C.stb[:, sl_], gsb[:, ft:ft + 1],
                    bsb[:, ft:ft + 1], ALU.mult, op1=ALU.add), deps=[a2, t_b1])
                pm.read(4, a1)
                pm.read(5, a2)
                ln_toks.append(a3)
            prod_toks = []
            prev_ep = []
            for tl in range(NTL):
                c0, ra = _pair_cols(tl)
                buf = tl % 2
                wtok[buf] = dma(b, "gpsimd",
                                wst[buf][:, 0:NKT * 2 * ra].rearrange(
                                    "p (kt f) -> p kt f", f=2 * ra),
                                w1p[li][:, c0:c0 + 2 * ra].rearrange(
                                    "(kt p) f -> p kt f", p=128),
                                deps=[wst_rd[buf]], sem=f"lwF{buf}")
                bank_a = 0 + tl % 2
                bank_g = 2 + tl % 2
                ga = acc_group(b, pm, bank_a,
                               [(wst[buf][:, k * 2 * ra: k * 2 * ra + ra],
                                 C.stb[:, k * 512:(k + 1) * 512]) for k in range(NKT)],
                               [wtok[buf]] + ln_toks, out=pm.t[bank_a][0:ra, :])
                ea = b.op("vector", lambda q, tl=tl, ra=ra, bank_a=bank_a:
                    q.tensor_scalar(
                        ab1[0:ra, AB1:AB1 + 512], pm.t[bank_a][0:ra, :],
                        b1sb[0:ra, 2 * tl:2 * tl + 1], None, ALU.add),
                    deps=[ga, t_b1] + prev_ep)
                pm.read(bank_a, ea)
                gg_ = acc_group(b, pm, bank_g,
                                [(wst[buf][:, k * 2 * ra + ra: k * 2 * ra + 2 * ra],
                                  C.stb[:, k * 512:(k + 1) * 512]) for k in range(NKT)],
                                [wtok[buf]] + ln_toks, out=pm.t[bank_g][0:ra, :])
                eg = b.op("scalar", lambda q, tl=tl, ra=ra, bank_g=bank_g: q.activation(
                    prodT[0:ra, tl * 512:(tl + 1) * 512], pm.t[bank_g][0:ra, :],
                    AF.Gelu, bias=b1sb[0:ra, 2 * tl + 1:2 * tl + 2]), deps=[gg_, t_b1])
                pm.read(bank_g, eg)
                ep = b.op("vector", lambda q, tl=tl, ra=ra: q.tensor_tensor(
                    prodT[0:ra, tl * 512:(tl + 1) * 512],
                    prodT[0:ra, tl * 512:(tl + 1) * 512],
                    ab1[0:ra, AB1:AB1 + 512], ALU.mult), deps=[ea, eg])
                prod_toks.append(ep)
                prev_ep = [ep]
                wst_rd[buf] = gg_
            y_toks = []
            for ft in range(NKT):
                buf = ft % 2
                w2dep = w2_rd[buf]
                w2tok2 = dma(b, "gpsimd",
                             v3(q2b, (128, 21, 128), (4 * 2048, 128, 1),
                                off=buf * NTL * 128),
                             w2in[li][0:2688, ft * 128:(ft + 1) * 128].rearrange(
                                 "(tl p) f -> p tl f", p=128),
                             deps=w2dep, sem=f"lw2F{buf}")
                w2tok3 = dma(b, "gpsimd",
                             q2b[0:42, buf * NTL * 128 + 21 * 128:
                                 buf * NTL * 128 + 22 * 128],
                             w2in[li][2688:2730, ft * 128:(ft + 1) * 128],
                             deps=w2dep, sem=f"lw2F{buf}")
                bank = 4 + ft % 2
                deps0 = pm.war(bank) + [w2tok2, w2tok3] + prod_toks
                tok = None
                for tl in range(NTL):
                    c0, ra = _pair_cols(tl)
                    tok = mm(b, pm.t[bank][:],
                             q2b[0:ra, buf * NTL * 128 + tl * 128:
                                 buf * NTL * 128 + (tl + 1) * 128],
                             prodT[0:ra, tl * 512:(tl + 1) * 512],
                             start=(tl == 0), stop=(tl == NTL - 1),
                             deps=deps0 if tl == 0 else [])
                pm.wrote(bank)
                w2_rd[buf] = [tok]
                e1 = b.op("vector", lambda q, ft=ft, bank=bank: q.tensor_scalar(
                    xn[:, ft * 512:(ft + 1) * 512], pm.t[bank][:],
                    b2sb[:, ft:ft + 1], None, ALU.add), deps=[tok, t_b1])
                e2 = b.op("vector", lambda q, ft=ft: q.tensor_tensor(
                    xn[:, ft * 512:(ft + 1) * 512], xn[:, ft * 512:(ft + 1) * 512],
                    xc[:, ft * 512:(ft + 1) * 512], ALU.add), deps=[e1] + ld)
                pm.read(bank, e2)
                y_toks.append(e2)
            out_store.append(dma(b, "sync",
                                 cur_out2[:, ch * 512:(ch + 1) * 512].rearrange(
                                     "(kt p) t -> p kt t", p=128),
                                 xn[:].rearrange("p (kt t) -> p kt t", t=512),
                                 deps=y_toks, sem=f"sout{li}"))
        cur_in = cur_out2

    if final_ln:
        fdep = [b.now("q_tensor"), b.now("q_vector"), b.now("q_scalar"),
                b.now("q_gpsimd")] + out_store[-1:]
        prev = []
        for ch in range(8):
            cdep = fdep + [b.now("q_tensor"), b.now("q_vector"), b.now("q_scalar")] + prev
            t_z = dma(b, "sync", xc[:].rearrange("p (kt t) -> p kt t", t=512),
                      cur_in[:, ch * 512:(ch + 1) * 512].rearrange(
                          "(kt p) t -> p kt t", p=128), deps=cdep, sem="lfz")
            t_x3 = dma(b, "sync", xn[:].rearrange("p (kt t) -> p kt t", t=512),
                       x3in[:, ch * 512:(ch + 1) * 512].rearrange(
                           "(kt p) t -> p kt t", p=128), deps=cdep, sem="lfz")
            t_z = t_x3
            s_toks = [b.op("vector", lambda q, ft=ft: q.tensor_tensor(
                xc[:, ft * 512:(ft + 1) * 512], xc[:, ft * 512:(ft + 1) * 512],
                xn[:, ft * 512:(ft + 1) * 512], ALU.add), deps=[t_z, t_x3])
                for ft in range(NKT)]
            bm, br = ln_stats(b, pm, C, xc[:], s_toks, 7, 4, 5)
            o_toks = []
            for ft in range(NKT):
                sl_ = slice(ft * 512, (ft + 1) * 512)
                a1 = b.op("vector", lambda q, sl_=sl_: q.tensor_tensor(
                    xn[:, sl_], xc[:, sl_], pm.t[4][:], ALU.subtract),
                    deps=[bm] + s_toks)
                a2 = b.op("vector", lambda q, sl_=sl_: q.tensor_tensor(
                    xn[:, sl_], xn[:, sl_], pm.t[5][:], ALU.mult), deps=[br, a1])
                pm.read(4, a1)
                pm.read(5, a2)
                o_toks.append(a2)
            prev = [dma(b, "sync", xout[:, ch * 512:(ch + 1) * 512].rearrange(
                "(kt p) t -> p kt t", p=128),
                xn[:].rearrange("p (kt t) -> p kt t", t=512), deps=o_toks, sem="sfin")]
    b.op("sync", lambda q: q.nop(), deps=b.barrier_tokens())
    b.build_block()
    return nc, es


def _f_tables():
    import ml_dtypes
    DH, W = 64, 256
    half = np.arange(0, DH, 2, dtype=np.float64)
    inv_freq = 1.0 / 10000.0 ** (half / DH)
    t = np.arange(T, dtype=np.float64)
    fr = t[:, None] * inv_freq[None, :]
    fr = np.concatenate([fr, fr], -1)
    base_scale = (half + 0.4 * DH) / (1.4 * DH)
    power = (t - T // 2) / (W // 2)
    scl = base_scale[None, :] ** power[:, None]
    scl = np.concatenate([scl, scl], -1)
    cos, sin = np.cos(fr), np.sin(fr)
    def tile(a):
        return np.ascontiguousarray(
            a.reshape(32, 128, 64).transpose(1, 0, 2).reshape(128, 32 * 64)
        ).astype(ml_dtypes.bfloat16)
    tb = [tile(cos * scl), tile(sin * scl), tile(cos / scl), tile(sin / scl)]
    tabs = np.concatenate(
        [np.concatenate([t[:, ch * 256:(ch + 1) * 256] for t in tb], axis=1)
         for ch in range(8)], axis=1)
    m = np.zeros((128, 4 * 256), np.float32)
    for ktr in range(4):
        kl = np.arange(128)[:, None]
        ql = np.arange(256)[None, :]
        d = ql - (kl + ktr * 128 - 256)
        bad = (d < 0) | (d > 256)
        m[:, ktr * 256:(ktr + 1) * 256] = np.where(bad, 0.0, 1.0)
    return tabs, m


def prep_F(inputs, x_in_key, resPrev, x3_src, layers):
    identb, identf, utrib = _consts()
    tabs, masks = _f_tables()
    n = len(layers)
    com = {"tabs": tabs, "masks": masks, "identb": identb}
    for j, i in enumerate(layers):
        W1 = inputs["L1_ffn_W1"][i]
        cols = []
        bvals = []
        b1 = inputs["L1_ffn_b1"][i]
        for tl in range(NTL):
            r0 = tl * 128
            ra = 128 if tl < 21 else 42
            cols.append(W1[:, r0:r0 + ra])
            cols.append(W1[:, DI + r0:DI + r0 + ra])
            ca = np.zeros(128, np.float32); ca[:ra] = b1[r0:r0 + ra]
            cg = np.zeros(128, np.float32); cg[:ra] = b1[DI + r0:DI + r0 + ra]
            bvals += [ca, cg]
        w1perm = np.ascontiguousarray(np.concatenate(cols, axis=1))
        b1cv = np.ascontiguousarray(np.stack(bvals, axis=1))
        com[f"wqkv{j}"] = inputs["L1_ffn_Wqkv"][i]
        com[f"wo{j}"] = inputs["L1_ffn_Wo"][i]
        com[f"w1p{j}"] = w1perm
        com[f"w2{j}"] = inputs["L1_ffn_W2"][i]
        com[f"b1c{j}"] = b1cv
        com[f"b2c{j}"] = np.ascontiguousarray(
            inputs["L1_ffn_b2"][i].reshape(NKT, 128).T)
        com[f"gci{j}"] = np.ascontiguousarray(
            inputs["L1_ffn_ln_g"][i].reshape(NKT, 128).T)
        com[f"bci{j}"] = np.ascontiguousarray(
            inputs["L1_ffn_ln_b"][i].reshape(NKT, 128).T)
        com[f"qsi{j}"] = np.ascontiguousarray(
            np.tile(inputs["L1_ffn_q_scale"][i][None, :], (128, 1)))
        com[f"ksi{j}"] = np.ascontiguousarray(
            np.tile(inputs["L1_ffn_k_scale"][i][None, :], (128, 1)))
    return [dict(com, xin=resPrev[c][x_in_key], x3Tin=x3_src[c])
            for c in range(len(resPrev))]


def _consts():
    import ml_dtypes
    identb = np.eye(128, dtype=ml_dtypes.bfloat16)
    identf = np.eye(128, dtype=np.float32)
    utrib = np.triu(np.ones((128, 128), np.float32), 1).astype(ml_dtypes.bfloat16)
    return identb, identf, utrib


def prep_A(inputs):
    identb, identf, utrib = _consts()
    x = inputs["L0_attn_x"]
    com = {
        "wqT": np.ascontiguousarray(inputs["L0_attn_Wq"].T),
        "wkvdT": np.ascontiguousarray(inputs["L0_attn_Wkv_down"].T),
        "wkupT": np.ascontiguousarray(inputs["L0_attn_Wk_up"].T),
        "wvupT": np.ascontiguousarray(inputs["L0_attn_Wv_up"].T),
        "woT": np.ascontiguousarray(inputs["L0_attn_Wo"].T),
        "g_col": np.ascontiguousarray(inputs["L0_ffn_ln_g"].reshape(NKT, 128).T),
        "b_col": np.ascontiguousarray(inputs["L0_ffn_ln_b"].reshape(NKT, 128).T),
        "wr": np.ascontiguousarray(inputs["L0_ffn_Wr"].reshape(NKT, 128, 7)
                                   .transpose(1, 0, 2).reshape(128, NKT * 7)),
        "identb": identb, "identf": identf, "utrib": utrib,
    }
    return [dict(com, xT=np.ascontiguousarray(x[c].T)) for c in range(NCORES)]


_cache = {}


def run_A(inputs):
    if "A" not in _cache:
        _cache["A"] = build_launch_A()
    nc, es = _cache["A"]
    res = run_bass_kernel_spmd(nc, prep_A(inputs), core_ids=list(range(NCORES)))
    return res.results


def _run(key, build_fn, maps):
    if key not in _cache:
        _cache[key] = build_fn()
    nc, es = _cache[key]
    return run_bass_kernel_spmd(nc, maps, core_ids=list(range(NCORES))).results


def kernel(**inputs):
    resA = run_A(inputs)
    resB = _run("B", build_launch_B, prep_B(inputs, resA))
    resC = _run("C", build_launch_C, prep_C(inputs, resA, resB))
    resD = _run("D", build_launch_D, prep_D(inputs, resC))
    resE = _run("E", build_launch_E, prep_E(resC, resD))
    mapsF = prep_F(inputs, "x3T", resE, [r["x3T"] for r in resE], list(range(6)))
    resF = _run("F", lambda: build_launch_F(6, True), mapsF)
    out = np.stack([r["xoutT"].T for r in resF], axis=0)
    return np.ascontiguousarray(out.astype(np.float32))
